# version 16
# baseline (speedup 1.0000x reference)
import numpy as np
import ml_dtypes
from collections import deque
from contextlib import ExitStack

import concourse.bass as bass
import concourse.mybir as mybir
from concourse.bass_utils import run_bass_kernel_spmd

F32 = mybir.dt.float32
BF16 = mybir.dt.bfloat16
AX = mybir.AxisListType.X
OP = mybir.AluOpType
AF = mybir.ActivationFunctionType

D = 1024
S = 4096
R = 2048
NE = 8
D2 = 2048
EPS = 1e-5

ENGINES = ("tensor", "vector", "scalar", "gpsimd", "sync")


class Plan:
    def __init__(self):
        self.q = {e: [] for e in ENGINES}
        self.cnt = {}

    def add(self, eng, fn, waits=(), inc=None, k=1):
        self.q[eng].append((tuple(waits), fn, inc, k))
        if inc is not None:
            self.cnt[inc] = self.cnt.get(inc, 0) + k
            return self.cnt[inc]
        return None

    def val(self, sem):
        return self.cnt.get(sem, 0)


def simulate(plan):
    idx = {e: 0 for e in ENGINES}
    sem = {}
    progress = True
    while progress:
        progress = False
        for e in ENGINES:
            while idx[e] < len(plan.q[e]):
                waits, _, inc, k = plan.q[e][idx[e]]
                if all(sem.get(s, 0) >= v for s, v in waits):
                    if inc is not None:
                        sem[inc] = sem.get(inc, 0) + k
                    idx[e] += 1
                    progress = True
                else:
                    break
    stuck = {e: (idx[e], len(plan.q[e])) for e in ENGINES if idx[e] < len(plan.q[e])}
    if stuck:
        det = {}
        for e in stuck:
            det[e] = [(s, v, sem.get(s, 0)) for s, v in plan.q[e][idx[e]][0]]
        raise RuntimeError(f"deadlock in plan: stuck={stuck} waits={det}")


class PsumPool:
    def __init__(self, tensors):
        self.items = [[t, None] for t in tensors]
        self.i = 0

    def take(self):
        it = self.items[self.i]
        self.i = (self.i + 1) % len(self.items)
        return it[0], ([it[1]] if it[1] is not None else [])

    def release(self, t, sem, val):
        for it in self.items:
            if it[0] is t:
                it[1] = (sem, val)
                return
        raise KeyError


def build_nc():
    nc = bass.Bass("TRN2", target_bir_lowering=False)

    d_x = nc.dram_tensor("x_full", [S, D], F32, kind="ExternalInput")
    d_xr = nc.dram_tensor("x_res", [R, D], F32, kind="ExternalInput")
    d_qkvT = nc.dram_tensor("qkvT", [D, 2560], F32, kind="ExternalInput")
    d_peT = nc.dram_tensor("peT", [D, D], F32, kind="ExternalInput")
    d_owT = nc.dram_tensor("owT", [D, D], F32, kind="ExternalInput")
    d_taps = nc.dram_tensor("taps", [D, 3], F32, kind="ExternalInput")
    d_wrT = nc.dram_tensor("wrT", [NE, D], F32, kind="ExternalInput")
    d_br = nc.dram_tensor("br", [1, NE], F32, kind="ExternalInput")
    d_b1 = nc.dram_tensor("b1", [NE, D2], F32, kind="ExternalInput")
    d_b2 = nc.dram_tensor("b2", [NE, D], F32, kind="ExternalInput")
    d_w1 = nc.dram_tensor("w1", [NE, D, D2], F32, kind="ExternalInput")
    d_w2 = nc.dram_tensor("w2", [NE, D2, D], F32, kind="ExternalInput")
    d_eye = nc.dram_tensor("eye", [128, 128], BF16, kind="ExternalInput")
    d_out = nc.dram_tensor("out", [R, D], F32, kind="ExternalOutput")

    st = ExitStack()
    NB = 89760
    BB = st.enter_context(nc.sbuf_tensor("BB", [128, NB], BF16))
    NF = 7488
    FB = st.enter_context(nc.sbuf_tensor("FB", [128, NF], F32))

    QKV_O = 0
    XNFM_O = 0
    AFM_O = 16384
    XLO_O = 17408
    CWF_O = 18432
    PEO_O = 20480
    XSR_O = 28672
    OWT_O = XSR_O
    ATW_O = 36864
    XNF_O = 53248
    ACC_O = 69632
    K_OS = (XNF_O, XNF_O + 2048)
    Q_OS = (XNF_O + 4096, XNF_O + 6144)
    PT_O = XNF_O + 8192
    V_OS = (XNF_O + 12288, ACC_O + 9216)
    W1_O = XNF_O
    XNR_O = ACC_O
    XSB_O = ACC_O + 6144
    XNT_O = ACC_O + 8192
    OSB_O = ACC_O + 11296
    OSB32_O = ACC_O + 13856
    XHI_O = ACC_O + 11808
    XLOT_O = ACC_O + 12832
    assert XLOT_O + 1024 <= 86016
    CST_O = 86016
    EYE_O = CST_O
    ONE_O = CST_O + 128
    WRH_O = CST_O + 256
    WRL_O = CST_O + 320
    B2_O = CST_O + 384
    B1C_O = CST_O + 1408
    B1N_O = CST_O + 1536
    BRB_O = CST_O + 3584
    CWB_O = CST_O + 3592
    assert CST_O + 3720 <= NB

    XT_O = 0
    XS0_O = 3072
    XNF32_O = 5120
    LNS_O = 6144
    RTR_O = 6208
    CWT_O = 6272
    TAP_O = 6400
    RL_O = 6424
    WRN2_O = 6432
    EYE32_O = WRN2_O
    LGT_O = WRN2_O + 128
    RSC_O = WRN2_O + 256
    EPSC_O = 7456
    assert EPSC_O + 1 <= NF

    def bb(off, w, p0=0, p1=128):
        return BB[p0:p1, off:off + w]

    def fb(off, w, p0=0, p1=128):
        return FB[p0:p1, off:off + w]

    def osb32(col0, w, p0=0, p1=128):
        return BB[p0:p1, OSB32_O + 2 * col0:OSB32_O + 2 * (col0 + w)].bitcast(F32)

    ps_st0 = st.enter_context(nc.psum_tensor("ps_st0", [128, 1024], F32))
    ps_st1 = st.enter_context(nc.psum_tensor("ps_st1", [128, 1024], F32))
    ps_o0 = st.enter_context(nc.psum_tensor("ps_o0", [128, 512], F32))
    ps_m0 = st.enter_context(nc.psum_tensor("ps_m0", [128, 512], F32))
    ps_m1 = st.enter_context(nc.psum_tensor("ps_m1", [128, 512], F32))
    ps_t0 = st.enter_context(nc.psum_tensor("ps_t0", [128, 256], BF16))

    sem_names = ["x", "w", "g", "te", "act", "dve", "out"]
    sems = {n: st.enter_context(nc.semaphore(n)) for n in sem_names}

    plan = Plan()
    pool_m = PsumPool([ps_m0, ps_m1])
    pool_tr = PsumPool([ps_t0])
    pool_y = PsumPool(["y0", "y1", "y2", "y3"])

    def ybase(key):
        i = int(key[1])
        return (ps_st0 if i < 2 else ps_st1), (i % 2) * 512

    def add(eng, fn, waits=(), inc=None, k=1):
        ws = [(s, v) for (s, v) in waits if v is not None and v > 0]
        return plan.add(eng, fn, ws, inc, k)

    pending = deque()

    def push(fn, tag=None):
        pending.append((tag, fn))

    def inject(n=1):
        for _ in range(n):
            if pending:
                pending.popleft()[1]()

    def flush():
        while pending:
            pending.popleft()[1]()

    def drain_tag(tag):
        while any(t == tag for t, _ in pending):
            pending.popleft()[1]()

    add("gpsimd", lambda e: e.dma_start(bb(EYE_O, 128), d_eye[:, :]), inc="w", k=16)
    add("gpsimd", lambda e: e.dma_start(
        bass.AP(BB, PEO_O, [[NB, 128], [1024, 8], [1, 1024]]),
        bass.AP(d_peT, 0, [[1024, 128], [1024 * 128, 8], [1, 1024]])), inc="w", k=16)
    mark_w_pe = plan.val("w")
    add("gpsimd", lambda e: e.dma_start(
        fb(TAP_O, 24),
        bass.AP(d_taps, 0, [[3, 128], [3 * 128, 8], [1, 3]]),
        allow_slow_non_contiguous=True), inc="w", k=16)
    add("gpsimd", lambda e: e.memset(bass.AP(BB, V_OS[0] + 64, [[NB, 128], [65, 32], [1, 1]]), 1.0),
        inc="g")
    add("gpsimd", lambda e: e.memset(bass.AP(BB, V_OS[1] + 64, [[NB, 128], [65, 32], [1, 1]]), 1.0),
        inc="g")
    add("gpsimd", lambda e: e.memset(bb(ONE_O, 128, 0, 1), 1.0), inc="g")
    add("gpsimd", lambda e: e.memset(fb(EPSC_O, 1), EPS), inc="g")
    mark_g_ones = plan.val("g")
    add("gpsimd", lambda e: e.dma_start(
        bass.AP(BB, QKV_O, [[NB, 128], [2560, 8], [1, 2560]]),
        bass.AP(d_qkvT, 0, [[2560, 128], [2560 * 128, 8], [1, 2560]])), inc="w", k=16)
    mark_w_qkv = plan.val("w")
    add("gpsimd", lambda e: e.dma_start(fb(WRN2_O, 1024, 0, 8), d_wrT[:, :]), inc="w", k=16)
    add("gpsimd", lambda e: e.dma_start(bb(BRB_O, 8, 0, 1), d_br[:, :]), inc="w", k=16)
    add("gpsimd", lambda e: e.dma_start(bb(B1N_O, 2048, 0, 8), d_b1[:, :]), inc="w", k=16)
    add("gpsimd", lambda e: e.dma_start(bb(B2_O, 1024, 0, 8), d_b2[:, :]), inc="w", k=16)
    mark_w_small = plan.val("w")

    WRHN_O = PT_O
    WRLN_O = PT_O + 1024
    add("vector", lambda e: e.tensor_copy(bb(WRHN_O, 1024, 0, 8), fb(WRN2_O, 1024, 0, 8)),
        waits=[("w", mark_w_small)])
    add("vector", lambda e: e.drain())
    add("vector", lambda e: e.tensor_tensor(bb(WRLN_O, 1024, 0, 8), fb(WRN2_O, 1024, 0, 8),
                                            bb(WRHN_O, 1024, 0, 8), OP.subtract), inc="dve")
    mark_dve_wr = plan.val("dve")
    mark_eye32 = add("scalar", lambda e: e.copy(fb(EYE32_O, 128), bb(EYE_O, 128)),
                     waits=[("dve", mark_dve_wr), ("w", mark_w_pe)], inc="act")
    for src_off, dst_off in ((WRHN_O, WRH_O), (WRLN_O, WRL_O)):
        for c in range(8):
            pt, w = pool_tr.take()
            mk = add("tensor", (lambda e, pt=pt, so=src_off, c=c: e.transpose(
                pt[0:128, 0:8], bb(so + c * 128, 128, 0, 8), bb(EYE_O, 8, 0, 8))),
                waits=w + [("dve", mark_dve_wr), ("w", mark_w_pe)], inc="te")
            dv = add("vector", (lambda e, pt=pt, do=dst_off, c=c: e.tensor_copy(
                bb(do + c * 8, 8), pt[0:128, 0:8])), waits=[("te", mk)], inc="dve")
            pool_tr.release(pt, "dve", dv)
    for d2c in range(16):
        pt, w = pool_tr.take()
        mk = add("tensor", (lambda e, pt=pt, d2c=d2c: e.transpose(
            pt[0:128, 0:8], bb(B1N_O + d2c * 128, 128, 0, 8), bb(EYE_O, 8, 0, 8))),
            waits=w + [("w", mark_w_small)], inc="te")
        dv = add("vector", (lambda e, pt=pt, d2c=d2c: e.tensor_copy(
            bb(B1C_O + d2c * 8, 8), pt[0:128, 0:8])), waits=[("te", mk)], inc="dve")
        pool_tr.release(pt, "dve", dv)

    mark_x = {}
    mark_act_apply = {}
    mark_xn_evac = {}
    mark_xs = {}
    mark_pe_mm = {}
    mark_cast = {}
    mark_qkv_mm = {}
    mark_kqv = {}
    mark_qrep = {}
    mark_xnt_tr = {}
    mark_attn_done = {}
    mark_router_apply = {}
    mark_cw = {}
    mark_xnfm = {}
    mark_rtr_mex = {}
    mark_logits = {}

    def push_p1_tile(g, t):
        def u_load():
            slot = t % 3
            prev = t - 3
            waits = [("act", mark_act_apply.get(prev))] if prev >= 0 else []
            mark_x[t] = add("sync", (lambda e: e.dma_start(
                fb(XT_O + slot * 1024, 1024), d_x[t * 128:(t + 1) * 128, :])),
                waits=waits, inc="x", k=16)
        push(u_load)

        def u_ln():
            slot = t % 3
            b = t % 2
            add("vector", (lambda e: e.bn_stats(
                fb(LNS_O + b * 16, 6), fb(XT_O + slot * 1024, 512))),
                waits=[("x", mark_x[t]), ("act", mark_act_apply.get(t - 2))])
            add("vector", (lambda e: e.bn_stats(
                fb(LNS_O + b * 16 + 6, 6), fb(XT_O + slot * 1024 + 512, 512))))
            add("vector", lambda e: e.drain())
            mdve = add("vector", (lambda e: e.bn_aggr(
                fb(LNS_O + 32 + b * 2, 2), fb(LNS_O + b * 16, 12))), inc="dve")
            add("scalar", (lambda e: e.activation(
                fb(LNS_O + 36 + b * 3, 1), fb(LNS_O + 32 + b * 2 + 1, 1),
                AF.Ln, bias=fb(EPSC_O, 1), scale=1.0)),
                waits=[("dve", mdve), ("g", mark_g_ones)])
            add("scalar", lambda e: e.drain())
            mact = add("scalar", (lambda e: e.activation(
                fb(LNS_O + 37 + b * 3, 1), fb(LNS_O + 36 + b * 3, 1),
                AF.Exp, bias=0.0, scale=-0.5)), inc="act")
            mdve = add("vector", (lambda e: e.scalar_tensor_tensor(
                fb(LNS_O + 38 + b * 3, 1), fb(LNS_O + 32 + b * 2, 1), -1.0,
                fb(LNS_O + 37 + b * 3, 1), OP.mult, OP.mult)),
                waits=[("act", mact)], inc="dve")
            mark_act_apply[t] = add("scalar", (lambda e: e.activation(
                bb(XNT_O, 1024), fb(XT_O + slot * 1024, 1024), AF.Identity,
                bias=fb(LNS_O + 38 + b * 3, 1), scale=fb(LNS_O + 37 + b * 3, 1))),
                waits=[("dve", mdve), ("te", mark_xnt_tr.get(t - 1))], inc="act")
        push(u_ln)

        for cc in range(8):
            def u_tr(c=cc):
                s3 = g % 3
                pt, w = pool_tr.take()
                mk = add("tensor", (lambda e, pt=pt, c=c: e.transpose(
                    pt[:, 0:128], bb(XNT_O + c * 128, 128), bb(EYE_O, 128))),
                    waits=w + [("act", mark_act_apply[t])], inc="te")
                if c == 7:
                    mark_xnt_tr[t] = mk
                dv = add("vector", (lambda e, pt=pt, c=c: e.tensor_copy(
                    bb(XNR_O + s3 * 2048 + c * 256 + (t % 2) * 128, 128), pt[:, 0:128])),
                    waits=[("te", mk)], inc="dve")
                pool_tr.release(pt, "dve", dv)
                if c == 7 and t % 2 == 1:
                    mark_xn_evac[g] = plan.val("dve")
            push(u_tr)

    def push_p1_conv_pe(g):
        def u_conv():
            sA, sL, sR = g % 3, (g - 1) % 3, (g + 1) % 3
            waits = [("dve", mark_xn_evac[min(g + 1, 15)])]
            if g >= 1:
                waits.append(("te", mark_pe_mm.get(g - 1)))
                waits.append(("act", mark_cast.get(g - 1)))
            for c in range(8):
                add("vector", (lambda e, c=c: e.tensor_scalar(
                    fb(XS0_O + c * 256, 256), bb(XNR_O + sA * 2048 + c * 256, 256),
                    fb(TAP_O + c * 3 + 1, 1), None, OP.mult)),
                    waits=waits if c == 0 else [])
            for c in range(8):
                add("vector", (lambda e, c=c: e.scalar_tensor_tensor(
                    fb(XS0_O + c * 256 + 1, 255), bb(XNR_O + sA * 2048 + c * 256, 255),
                    fb(TAP_O + c * 3 + 0, 1), fb(XS0_O + c * 256 + 1, 255), OP.mult, OP.add)))
            if g > 0:
                for c in range(8):
                    add("vector", (lambda e, c=c: e.scalar_tensor_tensor(
                        fb(XS0_O + c * 256, 1), bb(XNR_O + sL * 2048 + c * 256 + 255, 1),
                        fb(TAP_O + c * 3 + 0, 1), fb(XS0_O + c * 256, 1), OP.mult, OP.add)))
            for c in range(8):
                add("vector", (lambda e, c=c: e.scalar_tensor_tensor(
                    fb(XS0_O + c * 256, 255), bb(XNR_O + sA * 2048 + c * 256 + 1, 255),
                    fb(TAP_O + c * 3 + 2, 1), fb(XS0_O + c * 256, 255), OP.mult, OP.add)))
            mconv = None
            if g < 15:
                for c in range(8):
                    mconv = add("vector", (lambda e, c=c: e.scalar_tensor_tensor(
                        fb(XS0_O + c * 256 + 255, 1), bb(XNR_O + sR * 2048 + c * 256, 1),
                        fb(TAP_O + c * 3 + 2, 1), fb(XS0_O + c * 256 + 255, 1),
                        OP.mult, OP.add)), inc="dve")
            else:
                mconv = add("vector", lambda e: e.engine_nop(), inc="dve")
            mark_cast[g] = add("scalar", (lambda e: e.activation(
                bb(XSB_O, 2048), fb(XS0_O, 2048), AF.Identity, bias=0.0, scale=1.0)),
                waits=[("dve", mconv), ("te", mark_pe_mm.get(g - 1))], inc="act")
        push(u_conv)

        for o in range(8):
            def u_pe(o=o):
                slot4 = g % 4
                h_old = g - 4
                ps, w = pool_m.take()
                mk = None
                for ifc in range(8):
                    mk = add("tensor", (lambda e, ps=ps, ifc=ifc: e.matmul(
                        ps[:, 0:256], bb(PEO_O + ifc * 1024 + o * 128, 128),
                        bb(XSB_O + ifc * 256, 256), start=(ifc == 0), stop=(ifc == 7))),
                        waits=(w + [("act", mark_cast[g]), ("w", mark_w_pe)]) if ifc == 0 else [],
                        inc="te" if ifc == 7 else None)
                if o == 7:
                    mark_pe_mm[g] = mk
                ew = [("te", mk)]
                if h_old >= 0 and o == 0:
                    ew.append(("te", mark_qkv_mm.get(h_old)))
                dv = add("vector", (lambda e, ps=ps: e.tensor_add(
                    bb(XSR_O + slot4 * 2048 + o * 256, 256), ps[:, 0:256],
                    fb(XS0_O + o * 256, 256))), waits=ew, inc="dve")
                pool_m.release(ps, "dve", dv)
                if o == 7:
                    mark_xs[g] = plan.val("dve")
            push(u_pe)

    def push_p2_qkv(h):
        xb = XSR_O + (h % 4) * 2048
        ko, qo, vo = K_OS[h % 2], Q_OS[h % 2], V_OS[h % 2]
        for m in range(8):
            def u_k(m=m):
                ps, w = pool_m.take()
                mk = None
                for ifc in range(8):
                    mk = add("tensor", (lambda e, ps=ps, ifc=ifc, m=m: e.matmul(
                        ps[:, 0:256], bb(QKV_O + ifc * 2560 + 512 + m * 128, 128),
                        bb(xb + ifc * 256, 256), start=(ifc == 0), stop=(ifc == 7))),
                        waits=(w + [("dve", mark_xs[h]), ("w", mark_w_qkv)]) if ifc == 0 else [],
                        inc="te" if ifc == 7 else None)
                dv = add("vector", (lambda e, ps=ps, m=m: e.tensor_copy(
                    bb(ko + m * 256, 256), ps[:, 0:256])),
                    waits=[("te", mk), ("te", mark_attn_done.get(h - 2))], inc="dve")
                pool_m.release(ps, "dve", dv)
            push(u_k)
        for al in range(8):
            def u_q(al=al):
                ps, w = pool_m.take()
                mk = None
                for ifc in range(8):
                    mk = add("tensor", (lambda e, ps=ps, ifc=ifc: e.matmul(
                        ps[0:64, 0:256], bb(QKV_O + ifc * 2560 + al * 64, 64),
                        bb(xb + ifc * 256, 256), start=(ifc == 0), stop=(ifc == 7))),
                        waits=w if ifc == 0 else [], inc="te" if ifc == 7 else None)
                dv = add("vector", (lambda e, ps=ps: e.tensor_copy(
                    bb(qo + al * 256, 256, 0, 64), ps[0:64, 0:256])),
                    waits=[("te", mk), ("te", mark_attn_done.get(h - 2))], inc="dve")
                pool_m.release(ps, "dve", dv)
                if al == 7:
                    mark_qe = plan.val("dve")
                    mark_qrep[h] = add("gpsimd", (lambda e: e.dma_start(
                        bb(qo, 2048, 64, 128), bb(qo, 2048, 0, 64))),
                        waits=[("dve", mark_qe), ("te", mark_attn_done.get(h - 2))],
                        inc="w", k=16)
            push(u_q)
        for j in range(2):
            for vc in range(2):
                def u_v(j=j, vc=vc):
                    ps, w = pool_m.take()
                    mk = None
                    for ifc in range(8):
                        ww = []
                        if ifc == 0:
                            ww = w + ([("g", mark_g_ones)] if (h <= 1) else [])
                        mk = add("tensor", (lambda e, ps=ps, ifc=ifc: e.matmul(
                            ps[:, 0:512], bb(xb + ifc * 256 + j * 128, 128),
                            bb(QKV_O + ifc * 2560 + 1536 + vc * 512, 512),
                            start=(ifc == 0), stop=(ifc == 7))),
                            waits=ww, inc="te" if ifc == 7 else None)
                    mark_qkv_mm[h] = mk
                    dv = add("vector", (lambda e, ps=ps: e.tensor_copy(
                        bass.AP(BB, vo + j * 1040 + vc * 520, [[NB, 128], [65, 8], [1, 64]]),
                        ps[:, 0:512])),
                        waits=[("te", mk), ("te", mark_attn_done.get(h - 2))], inc="dve")
                    pool_m.release(ps, "dve", dv)
                    if j == 1 and vc == 1:
                        mark_kqv[h] = plan.val("dve")
                push(u_v)

    mark_o_free = [None, None]
    mark_st_free = [None]

    st_exp_mark = [None, None]
    av_hist = {}
    exp_hist = {}
    gpair = [0]
    mark_o_tr = [None]

    def p2_attn(h):
        ko, qo, vo = K_OS[h % 2], Q_OS[h % 2], V_OS[h % 2]
        for sqc in range(4):
            ob = 0
            ps_o = ps_o0
            base = gpair[0]
            for pp in range(16):
                k = base + pp
                pst = ps_st0 if k % 2 == 0 else ps_st1
                stw = [("act", st_exp_mark[k % 2])] if st_exp_mark[k % 2] else []
                if pp == 0:
                    stw.append(("dve", mark_kqv[h]))
                    stw.append(("w", mark_qrep[h]))
                m2 = pp // 2
                jj = pp % 2
                add("tensor", (lambda e, m2=m2, jj=jj, sqc=sqc, ko=ko, qo=qo,
                     pst=pst: e.matmul(
                    pst[:, 0:512], bb(ko + m2 * 256 + jj * 128, 128, 0, 64),
                    bb(qo + sqc * 512, 512, 0, 64), start=True, stop=True,
                    tile_position=(0, 0))),
                    waits=stw)
                tep = add("tensor", (lambda e, m2=m2, jj=jj, sqc=sqc, ko=ko, qo=qo,
                          pst=pst: e.matmul(
                    pst[:, 512:1024], bb(ko + m2 * 256 + jj * 128, 128, 64, 128),
                    bb(qo + sqc * 512, 512, 64, 128), start=True, stop=True,
                    tile_position=(64, 0))), inc="te")
                ew = [("te", tep)]
                if (k - 4) in av_hist:
                    ew.append(("te", av_hist[k - 4]))
                exp_hist[k] = add("scalar", (lambda e, k=k, pst=pst: e.activation(
                    bb(PT_O + (k % 4) * 1024, 1024), pst[:, 0:1024], AF.Exp,
                    bias=0.0, scale=0.125)), waits=ew, inc="act")
                st_exp_mark[k % 2] = exp_hist[k]
                mark_st_free[0] = exp_hist[k]
                inject(1)
                if pp >= 2:
                    q = pp - 2
                    kq = base + q
                    qm, qj = q // 2, q % 2
                    aw = [("act", exp_hist[kq])]
                    if q == 0 and mark_o_free[ob]:
                        aw.append(("dve", mark_o_free[ob]))
                    add("tensor", (lambda e, qm=qm, qj=qj, q=q, kq=kq, vo=vo,
                         ps_o=ps_o: e.matmul(
                        ps_o[0:65, 0:512], bb(vo + qj * 1040 + (2 * qm) * 65, 65),
                        bb(PT_O + (kq % 4) * 1024, 512), start=(q == 0), stop=False)),
                        waits=aw)
                    av_hist[kq] = add("tensor", (lambda e, qm=qm, qj=qj, kq=kq,
                                      vo=vo, ps_o=ps_o: e.matmul(
                        ps_o[0:65, 0:512], bb(vo + qj * 1040 + (2 * qm + 1) * 65, 65),
                        bb(PT_O + (kq % 4) * 1024 + 512, 512), start=False, stop=False)),
                        inc="te")
                inject(1)
            for q in (14, 15):
                kq = base + q
                qm, qj = q // 2, q % 2
                add("tensor", (lambda e, qm=qm, qj=qj, kq=kq, vo=vo, ps_o=ps_o: e.matmul(
                    ps_o[0:65, 0:512], bb(vo + qj * 1040 + (2 * qm) * 65, 65),
                    bb(PT_O + (kq % 4) * 1024, 512), start=False, stop=False)),
                    waits=[("act", exp_hist[kq])])
                av_hist[kq] = add("tensor", (lambda e, qm=qm, qj=qj, q=q, kq=kq,
                                  vo=vo, ps_o=ps_o: e.matmul(
                    ps_o[0:65, 0:512], bb(vo + qj * 1040 + (2 * qm + 1) * 65, 65),
                    bb(PT_O + (kq % 4) * 1024 + 512, 512), start=False,
                    stop=(q == 15))), inc="te")
            gpair[0] = base + 16
            drain_tag("ofin")
            dvo = add("vector", (lambda e, ps_o=ps_o: e.tensor_copy(
                osb32(0, 512, 0, 65), ps_o[0:65, 0:512])),
                waits=[("te", av_hist[base + 15]), ("te", mark_o_tr[0])], inc="dve")
            mark_o_free[ob] = dvo
            for q128 in range(4):
                def u_ofin(q128=q128, sqc=sqc, h=h, dvo=dvo):
                    pt, w = pool_m.take()
                    ti = sqc * 4 + q128
                    mk = add("tensor", (lambda e, pt=pt, q128=q128: e.transpose(
                        pt[:, 0:65], osb32(q128 * 128, 128, 0, 65),
                        fb(EYE32_O, 65, 0, 65))),
                        waits=w + [("dve", dvo), ("act", mark_eye32)], inc="te")
                    mark_o_tr[0] = mk
                    if q128 == 3:
                        mark_attn_done[h] = mk
                    add("vector", (lambda e, pt=pt, q128=q128: e.reciprocal(
                        fb(RL_O + q128 % 2, 1), pt[:, 64:65])), waits=[("te", mk)])
                    add("vector", lambda e: e.drain())
                    dv = add("vector", (lambda e, pt=pt, ti=ti, h=h, q128=q128: e.tensor_scalar(
                        bb(ATW_O + ti * 1024 + h * 64, 64), pt[:, 0:64],
                        fb(RL_O + q128 % 2, 1), None, OP.mult)), inc="dve")
                    pool_m.release(pt, "dve", dv)
                push(u_ofin, tag="ofin")

    def push_router(ti):
        def u_ld():
            slot = ti % 3
            waits = [("act", mark_act_apply[31])]
            if ti >= 3:
                waits = [("act", mark_router_apply[ti - 3])]
            mark_x[("r", ti)] = add("sync", (lambda e: e.dma_start(
                fb(XT_O + slot * 1024, 1024), d_xr[ti * 128:(ti + 1) * 128, :])),
                waits=waits, inc="x", k=16)
        push(u_ld)

        def u_ln():
            slot = ti % 3
            b = ti % 2
            add("vector", (lambda e: e.bn_stats(
                fb(LNS_O + b * 16, 6), fb(XT_O + slot * 1024, 512))),
                waits=[("x", mark_x[("r", ti)]),
                       ("act", mark_router_apply.get(ti - 2) if ti >= 2
                        else mark_act_apply.get(31))])
            add("vector", (lambda e: e.bn_stats(
                fb(LNS_O + b * 16 + 6, 6), fb(XT_O + slot * 1024 + 512, 512))))
            add("vector", lambda e: e.drain())
            mdve = add("vector", (lambda e: e.bn_aggr(
                fb(LNS_O + 32 + b * 2, 2), fb(LNS_O + b * 16, 12))), inc="dve")
            add("scalar", (lambda e: e.activation(
                fb(LNS_O + 36 + b * 3, 1), fb(LNS_O + 32 + b * 2 + 1, 1), AF.Ln,
                bias=fb(EPSC_O, 1), scale=1.0)), waits=[("dve", mdve)])
            add("scalar", lambda e: e.drain())
            mact = add("scalar", (lambda e: e.activation(
                fb(LNS_O + 37 + b * 3, 1), fb(LNS_O + 36 + b * 3, 1),
                AF.Exp, bias=0.0, scale=-0.5)), inc="act")
            mdve = add("vector", (lambda e: e.scalar_tensor_tensor(
                fb(LNS_O + 38 + b * 3, 1), fb(LNS_O + 32 + b * 2, 1), -1.0,
                fb(LNS_O + 37 + b * 3, 1), OP.mult, OP.mult)),
                waits=[("act", mact)], inc="dve")
            mark_router_apply[ti] = add("scalar", (lambda e: e.activation(
                fb(XNF32_O, 1024), fb(XT_O + slot * 1024, 1024), AF.Identity,
                bias=fb(LNS_O + 38 + b * 3, 1), scale=fb(LNS_O + 37 + b * 3, 1))),
                waits=[("dve", mdve), ("dve", mark_xnfm.get(ti - 1))], inc="act")
            add("vector", (lambda e: e.tensor_copy(bb(XHI_O, 1024), fb(XNF32_O, 1024))),
                waits=[("act", mark_router_apply[ti]), ("te", mark_attn_done.get(15))])
            add("vector", lambda e: e.drain())
            add("vector", (lambda e: e.tensor_tensor(
                bb(XLOT_O, 1024), fb(XNF32_O, 1024), bb(XHI_O, 1024), OP.subtract)),
                inc="dve")
        push(u_ln)

        for cc in range(8):
            def u_tr(c=cc):
                mhl = plan.val("dve")
                pt, w = pool_tr.take()
                mk = add("tensor", (lambda e, pt=pt, c=c: e.transpose(
                    pt[:, 0:128], bb(XHI_O + c * 128, 128), bb(EYE_O, 128))),
                    waits=w + [("dve", mhl), ("te", mark_qkv_mm[15])], inc="te")
                dv = add("vector", (lambda e, pt=pt, c=c: e.tensor_copy(
                    bb(XNFM_O + c * 2048 + ti * 128, 128), pt[:, 0:128])),
                    waits=[("te", mk)], inc="dve")
                pool_tr.release(pt, "dve", dv)
                pt, w = pool_tr.take()
                mk = add("tensor", (lambda e, pt=pt, c=c: e.transpose(
                    pt[:, 0:128], bb(XLOT_O + c * 128, 128), bb(EYE_O, 128))),
                    waits=w, inc="te")
                dv = add("vector", (lambda e, pt=pt, c=c: e.tensor_copy(
                    bb(XLO_O + c * 128, 128), pt[:, 0:128])), waits=[("te", mk)], inc="dve")
                pool_tr.release(pt, "dve", dv)
                if c == 7:
                    mark_xnfm[ti] = dv
            push(u_tr)

        def u_mm():
            mtrans = mark_xnfm[ti]
            ps, w = pool_m.take()
            first = True
            for (lo_lhs, rhs_off) in ((0, WRH_O), (0, WRL_O), (1, WRH_O)):
                for ifc in range(8):
                    lhs_off = (XLO_O + ifc * 128) if lo_lhs else (XNFM_O + ifc * 2048 + ti * 128)
                    add("tensor", (lambda e, ps=ps, lo=lhs_off, ro=rhs_off, ifc=ifc,
                                   fr=first: e.matmul(
                        ps[:, 0:8], bb(lo, 128), bb(ro + ifc * 8, 8),
                        start=fr, stop=False)),
                        waits=(w + [("dve", mtrans)]) if first else [])
                    first = False
            mk = add("tensor", (lambda e, ps=ps: e.matmul(
                ps[:, 0:8], bb(ONE_O, 128, 0, 1), bb(BRB_O, 8, 0, 1),
                start=False, stop=True)), inc="te")
            dv = add("vector", (lambda e, ps=ps, ti=ti: e.tensor_copy(
                fb(LGT_O + ti * 8, 8), ps[:, 0:8])), waits=[("te", mk)], inc="dve")
            pool_m.release(ps, "dve", dv)
            mark_logits[ti] = dv
        push(u_mm)

    for it in range(18):
        if it < 16:
            push_p1_tile(it, 2 * it)
            push_p1_tile(it, 2 * it + 1)
        if 1 <= it <= 16:
            push_p1_conv_pe(it - 1)
        if 0 <= it - 1 < 16:
            push_p2_qkv(it - 1)
        if it == 17:
            for ti in range(16):
                push_router(ti)
        hh = it - 2
        if 0 <= hh < 16:
            p2_attn(hh)
        flush()

    mark_attn_all = plan.val("dve")

    def seg(off, n=16, k=8, bcast=False):
        if bcast:
            return bass.AP(FB, off, [[NF, 128], [1, n], [0, k]])
        return bass.AP(FB, off, [[NF, 128], [8, n], [1, k]])

    def seg1(off, n=16):
        return bass.AP(FB, off, [[NF, 128], [1, n]])

    mlg = mark_logits[15]
    add("vector", (lambda e: e.tensor_reduce(
        seg1(RSC_O), seg(LGT_O), AX, OP.max, negate=True)),
        waits=[("dve", mlg)])
    add("vector", lambda e: e.drain())
    mdv = add("vector", (lambda e: e.tensor_tensor(
        fb(RSC_O + 64, 128), fb(LGT_O, 128), seg(RSC_O, bcast=True), OP.add)), inc="dve")
    mex = add("scalar", (lambda e: e.activation(
        fb(RSC_O + 64, 128), fb(RSC_O + 64, 128), AF.Exp, bias=0.0, scale=1.0)),
        waits=[("dve", mdv)], inc="act")
    add("vector", (lambda e: e.tensor_reduce(
        seg1(RSC_O + 16), seg(RSC_O + 64), AX, OP.add)), waits=[("act", mex)])
    add("vector", lambda e: e.drain())
    add("vector", (lambda e: e.reciprocal(fb(RSC_O + 32, 16), fb(RSC_O + 16, 16))))
    add("vector", lambda e: e.drain())
    add("vector", (lambda e: e.tensor_tensor(
        fb(RSC_O + 64, 128), fb(RSC_O + 64, 128), seg(RSC_O + 32, bcast=True), OP.mult)))
    add("vector", lambda e: e.drain())
    add("vector", (lambda e: e.tensor_reduce(
        seg1(RSC_O), seg(RSC_O + 64), AX, OP.max)))
    add("vector", lambda e: e.drain())
    add("vector", (lambda e: e.tensor_tensor(
        fb(CWT_O, 128), fb(RSC_O + 64, 128), seg(RSC_O, bcast=True), OP.is_ge)))
    add("vector", lambda e: e.drain())
    add("vector", (lambda e: e.scalar_tensor_tensor(
        fb(CWT_O, 128), fb(CWT_O, 128), -1e30, fb(RSC_O + 64, 128), OP.mult, OP.add)))
    add("vector", lambda e: e.drain())
    add("vector", (lambda e: e.tensor_reduce(
        seg1(RSC_O), seg(CWT_O), AX, OP.max)))
    add("vector", lambda e: e.drain())
    add("vector", (lambda e: e.tensor_tensor(
        fb(CWT_O, 128), fb(RSC_O + 64, 128), seg(RSC_O, bcast=True), OP.is_ge)))
    add("vector", lambda e: e.drain())
    mcw = add("vector", (lambda e: e.tensor_tensor(
        fb(CWT_O, 128), fb(CWT_O, 128), fb(RSC_O + 64, 128), OP.mult)), inc="dve")
    add("vector", lambda e: e.drain())
    mcwb = add("vector", (lambda e: e.tensor_copy(bb(CWB_O, 128), fb(CWT_O, 128))),
               inc="dve")
    for ti in range(16):
        pt, w = pool_tr.take()
        mk = add("tensor", (lambda e, pt=pt, ti=ti: e.transpose(
            pt[0:8, 0:128], bb(CWB_O + ti * 8, 8), bb(EYE_O, 128))),
            waits=w + [("dve", mcwb)], inc="te")
        dv = add("vector", (lambda e, pt=pt, ti=ti: e.tensor_copy(
            bb(CWF_O + ti * 128, 128, 0, 8), pt[0:8, 0:128])),
            waits=[("te", mk)], inc="dve")
        pool_tr.release(pt, "dve", dv)
        mark_cw[ti] = dv

    add("gpsimd", lambda e: e.dma_start(
        bass.AP(BB, OWT_O, [[NB, 128], [1024, 8], [1, 1024]]),
        bass.AP(d_owT, 0, [[1024, 128], [1024 * 128, 8], [1, 1024]])),
        waits=[("te", mark_qkv_mm[15])], inc="w", k=16)
    mark_w_ow = plan.val("w")

    for it_ in pool_y.items:
        it_[1] = ("act", mark_st_free[0])

    mark_p3_te_last = None
    mark_acc_last = {}
    mark_p3_trans_last = [None]

    def push_p3(ti):
        def u_p3():
            dvs = []
            for c in range(8):
                pt, w = pool_tr.take()
                mk = add("tensor", (lambda e, pt=pt, ti=ti, c=c: e.transpose(
                    pt[:, 0:128], bb(ATW_O + ti * 1024 + c * 128, 128), bb(EYE_O, 128))),
                    waits=w + ([("dve", mark_attn_all)] if (ti == 0 and c == 0) else []),
                    inc="te")
                mark_p3_trans_last[0] = mk
                dv = add("vector", (lambda e, pt=pt, c=c: e.tensor_copy(
                    bb(AFM_O + c * 128, 128), pt[:, 0:128])), waits=[("te", mk)], inc="dve")
                pool_tr.release(pt, "dve", dv)
                dvs.append(dv)
            for half in range(2):
                key, w = pool_y.take()
                yt, yb = ybase(key)
                mk = None
                for c in range(8):
                    mk = add("tensor", (lambda e, yt=yt, yb=yb, c=c, half=half: e.matmul(
                        yt[:, yb:yb + 512], bb(AFM_O + c * 128, 128),
                        bb(OWT_O + c * 1024 + half * 512, 512), start=(c == 0), stop=False)),
                        waits=(w + [("dve", dvs[-1]), ("w", mark_w_ow), ("dve", mark_cw[ti])])
                        if c == 0 else [])
                mk = add("tensor", (lambda e, yt=yt, yb=yb, ti=ti, half=half: e.matmul(
                    yt[:, yb:yb + 512], bb(CWF_O + ti * 128, 128, 0, 8),
                    bb(B2_O + half * 512, 512, 0, 8), start=False, stop=True)), inc="te")
                dv = add("vector", (lambda e, yt=yt, yb=yb, ti=ti, half=half: e.tensor_copy(
                    bb(ACC_O + ti * 1024 + half * 512, 512), yt[:, yb:yb + 512])),
                    waits=[("te", mk)], inc="dve")
                pool_y.release(key, "dve", dv)
                mark_acc_last[(ti, half)] = dv
        push(u_p3)

    for ti in range(16):
        push_p3(ti)

    mark_w1_mm_done = None
    mark_w2_mm_done = None
    mark_w1_load = None
    mark_w2_load = None
    first_w2 = True
    for ee in range(8):
        for s in [0]:
            w1_waits = [("te", mark_w1_mm_done)] if mark_w1_mm_done is not None else \
                [("te", mark_attn_done[15])]
            mark_w1_load = add("gpsimd", (lambda e, ee=ee: e.dma_start(
                bass.AP(BB, W1_O, [[NB, 128], [2048, 8], [1, 2048]]),
                bass.AP(d_w1, ee * D * D2, [[2048, 128], [2048 * 128, 8], [1, 2048]]))),
                waits=w1_waits, inc="w", k=16)
            if not first_w2:
                mark_w2_load = add("gpsimd", (lambda e, ee=ee: e.dma_start(
                    bass.AP(BB, ATW_O, [[NB, 128], [1024, 16], [1, 1024]]),
                    bass.AP(d_w2, ee * D2 * D, [[1024, 128], [1024 * 128, 16], [1, 1024]]))),
                    waits=[("te", mark_w2_mm_done)], inc="w", k=16)
            for tc in range(4):
                hmarks = []
                for d2c in range(16):
                    ps, w = pool_m.take()
                    mk = None
                    for ifc in range(8):
                        waits = []
                        if ifc == 0:
                            waits = w + [("w", mark_w1_load), ("dve", mark_xnfm.get(15))]
                        mk = add("tensor", (lambda e, ps=ps, d2c=d2c, ifc=ifc, s=s, tc=tc: e.matmul(
                            ps[:, 0:512], bb(W1_O + ifc * 2048 + d2c * 128, 128),
                            bb(XNFM_O + ifc * 2048 + tc * 512, 512),
                            start=(ifc == 0), stop=(ifc == 7))),
                            waits=waits, inc="te" if ifc == 7 else None)
                    mark_w1_mm_done = mk
                    if d2c % 2 == 1:
                        inject(1)
                    hw = [("te", mk)]
                    if mark_w2_mm_done is not None:
                        hw.append(("te", mark_w2_mm_done))
                    hm = add("scalar", (lambda e, ps=ps, d2c=d2c, ee=ee: e.activation(
                        bb(PEO_O + d2c * 512, 512), ps[:, 0:512], AF.Relu,
                        bias=bb(B1C_O + d2c * 8 + ee, 1))), waits=hw, inc="act")
                    pool_m.release(ps, "act", hm)
                    hmarks.append(hm)
                if first_w2:
                    flush()
                    first_w2 = False
                    mark_w2_load = add("gpsimd", (lambda e, ee=ee: e.dma_start(
                        bass.AP(BB, ATW_O, [[NB, 128], [1024, 16], [1, 1024]]),
                        bass.AP(d_w2, ee * D2 * D, [[1024, 128], [1024 * 128, 16], [1, 1024]]))),
                        waits=[("te", mark_p3_trans_last[0])], inc="w", k=16)
                for tj in range(4):
                    ti = tc * 4 + tj
                    for half in range(2):
                        key, w = pool_y.take()
                        yt, yb = ybase(key)
                        mk = None
                        for d2c in range(16):
                            waits = [("act", hmarks[d2c])]
                            if d2c == 0:
                                waits += w + [("w", mark_w2_load)]
                            mk = add("tensor", (lambda e, yt=yt, yb=yb, d2c=d2c, tj=tj,
                                     half=half: e.matmul(
                                yt[:, yb:yb + 512],
                                bb(PEO_O + d2c * 512 + tj * 128, 128),
                                bb(ATW_O + d2c * 1024 + half * 512, 512),
                                start=(d2c == 0), stop=(d2c == 15))),
                                waits=waits, inc="te" if d2c == 15 else None)
                        mark_w2_mm_done = mk
                        dv = add("vector", (lambda e, yt=yt, yb=yb, ti=ti, half=half, ee=ee:
                                 e.scalar_tensor_tensor(
                            bb(ACC_O + ti * 1024 + half * 512, 512),
                            yt[:, yb:yb + 512],
                            fb(CWT_O + ti * 8 + ee, 1), bb(ACC_O + ti * 1024 + half * 512, 512),
                            OP.mult, OP.add)),
                            waits=[("te", mk), ("dve", mark_acc_last[(ti, half)])], inc="dve")
                        mark_acc_last[(ti, half)] = dv
                        pool_y.release(key, "dve", dv)

    mark_out_stage = {}
    for ti in range(16):
        slot = ti % 3
        waits = [("act", mark_router_apply[15])]
        if ti >= 3:
            waits = [("dve", mark_out_stage[ti - 3])]
        mx = add("sync", (lambda e, ti=ti, slot=slot: e.dma_start(
            fb(XT_O + slot * 1024, 1024), d_xr[ti * 128:(ti + 1) * 128, :])),
            waits=waits, inc="x", k=16)
        ob = ti % 2
        dv = add("vector", (lambda e, slot=slot, ob=ob, ti=ti: e.tensor_tensor(
            fb(XS0_O + ob * 1024, 1024), fb(XT_O + slot * 1024, 1024),
            bb(ACC_O + ti * 1024, 1024), OP.add)),
            waits=[("x", mx), ("dve", mark_acc_last[(ti, 0)]), ("dve", mark_acc_last[(ti, 1)]),
                   ("out", None if ti < 2 else (ti - 1) * 16)],
            inc="dve")
        mark_out_stage[ti] = dv
        add("sync", (lambda e, ti=ti, ob=ob: e.dma_start(
            d_out[ti * 128:(ti + 1) * 128, :], fb(XS0_O + ob * 1024, 1024))),
            waits=[("dve", dv)], inc="out", k=16)
    add("sync", lambda e: e.nop(), waits=[("out", plan.val("out"))])

    simulate(plan)

    with st, nc.Block() as block:
        def replay(name):
            def f(eng):
                hw = {}
                for waits, fn, inc, k in plan.q[name]:
                    for sname, v in waits:
                        if hw.get(sname, 0) < v:
                            eng.wait_ge(sems[sname], v)
                            hw[sname] = v
                    inst = fn(eng)
                    if inc is not None:
                        inst.then_inc(sems[inc], k)
            return f

        block.tensor(replay("tensor"))
        block.vector(replay("vector"))
        block.scalar(replay("scalar"))
        block.gpsimd(replay("gpsimd"))
        block.sync(replay("sync"))

    return nc


_CACHE = {}


def kernel(L0_attn_x, L0_attn_conv_w, L0_attn_pe_w, L0_attn_qkv_w, L0_attn_out_w,
           L0_ffn_Wr, L0_ffn_br, L0_ffn_W1, L0_ffn_b1, L0_ffn_W2, L0_ffn_b2):
    x = np.ascontiguousarray(np.asarray(L0_attn_x, dtype=np.float32))
    qkv_w = np.asarray(L0_attn_qkv_w, dtype=np.float32)
    pe_wT = np.ascontiguousarray(np.asarray(L0_attn_pe_w, dtype=np.float32).T)
    out_wT = np.ascontiguousarray(np.asarray(L0_attn_out_w, dtype=np.float32).T)
    taps = np.ascontiguousarray(np.asarray(L0_attn_conv_w, dtype=np.float32)[:, 0, :, 1])
    wrT = np.ascontiguousarray(np.asarray(L0_ffn_Wr, dtype=np.float32).T)
    br = np.ascontiguousarray(np.asarray(L0_ffn_br, dtype=np.float32).reshape(1, NE))
    b1 = np.ascontiguousarray(np.asarray(L0_ffn_b1, dtype=np.float32))
    b2 = np.ascontiguousarray(np.asarray(L0_ffn_b2, dtype=np.float32))
    w1 = np.ascontiguousarray(np.asarray(L0_ffn_W1, dtype=np.float32))
    w2 = np.ascontiguousarray(np.asarray(L0_ffn_W2, dtype=np.float32))
    eye = np.eye(128, dtype=ml_dtypes.bfloat16)

    if "nc" not in _CACHE:
        _CACHE["nc"] = build_nc()
    nc = _CACHE["nc"]

    o = np.arange(R)
    in_maps = []
    idxs = []
    for c in range(8):
        b, p = c // 2, c % 2
        idx = 16 * (o % 256) + 8 * p + o // 256
        idxs.append((b, idx))
        qkvT_local = np.ascontiguousarray(
            np.concatenate([qkv_w[512 * p:512 * (p + 1)], qkv_w[1024:3072]], 0).T)
        in_maps.append({
            "x_full": np.ascontiguousarray(x[b]),
            "x_res": np.ascontiguousarray(x[b][idx]),
            "qkvT": qkvT_local,
            "peT": pe_wT, "owT": out_wT, "taps": taps, "wrT": wrT, "br": br,
            "b1": b1, "b2": b2, "w1": w1, "w2": w2, "eye": eye,
        })

    res = run_bass_kernel_spmd(nc, in_maps, core_ids=list(range(8)))
    out = np.empty((4, S, D), dtype=np.float32)
    for c in range(8):
        b, idx = idxs[c]
        out[b][idx] = np.asarray(res.results[c]["out"], dtype=np.float32)
    return out


# version 18
# speedup vs baseline: 1.0056x; 1.0056x over previous
import numpy as np
import ml_dtypes
from collections import deque
from contextlib import ExitStack

import concourse.bass as bass
import concourse.mybir as mybir
from concourse.bass_utils import run_bass_kernel_spmd

F32 = mybir.dt.float32
BF16 = mybir.dt.bfloat16
AX = mybir.AxisListType.X
OP = mybir.AluOpType
AF = mybir.ActivationFunctionType

D = 1024
S = 4096
R = 2048
NE = 8
D2 = 2048
EPS = 1e-5

ENGINES = ("tensor", "vector", "scalar", "gpsimd", "sync")


class Plan:
    def __init__(self):
        self.q = {e: [] for e in ENGINES}
        self.cnt = {}

    def add(self, eng, fn, waits=(), inc=None, k=1):
        self.q[eng].append((tuple(waits), fn, inc, k))
        if inc is not None:
            self.cnt[inc] = self.cnt.get(inc, 0) + k
            return self.cnt[inc]
        return None

    def val(self, sem):
        return self.cnt.get(sem, 0)


def simulate(plan):
    idx = {e: 0 for e in ENGINES}
    sem = {}
    progress = True
    while progress:
        progress = False
        for e in ENGINES:
            while idx[e] < len(plan.q[e]):
                waits, _, inc, k = plan.q[e][idx[e]]
                if all(sem.get(s, 0) >= v for s, v in waits):
                    if inc is not None:
                        sem[inc] = sem.get(inc, 0) + k
                    idx[e] += 1
                    progress = True
                else:
                    break
    stuck = {e: (idx[e], len(plan.q[e])) for e in ENGINES if idx[e] < len(plan.q[e])}
    if stuck:
        det = {}
        for e in stuck:
            det[e] = [(s, v, sem.get(s, 0)) for s, v in plan.q[e][idx[e]][0]]
        raise RuntimeError(f"deadlock in plan: stuck={stuck} waits={det}")


class PsumPool:
    def __init__(self, tensors):
        self.items = [[t, None] for t in tensors]
        self.i = 0

    def take(self):
        it = self.items[self.i]
        self.i = (self.i + 1) % len(self.items)
        return it[0], ([it[1]] if it[1] is not None else [])

    def release(self, t, sem, val):
        for it in self.items:
            if it[0] is t:
                it[1] = (sem, val)
                return
        raise KeyError


def build_nc():
    nc = bass.Bass("TRN2", target_bir_lowering=False)

    d_x = nc.dram_tensor("x_full", [S, D], F32, kind="ExternalInput")
    d_xr = nc.dram_tensor("x_res", [R, D], F32, kind="ExternalInput")
    d_qkvT = nc.dram_tensor("qkvT", [D, 2560], F32, kind="ExternalInput")
    d_peT = nc.dram_tensor("peT", [D, D], F32, kind="ExternalInput")
    d_owT = nc.dram_tensor("owT", [D, D], F32, kind="ExternalInput")
    d_taps = nc.dram_tensor("taps", [D, 3], F32, kind="ExternalInput")
    d_wrT = nc.dram_tensor("wrT", [NE, D], F32, kind="ExternalInput")
    d_br = nc.dram_tensor("br", [1, NE], F32, kind="ExternalInput")
    d_b1 = nc.dram_tensor("b1", [NE, D2], F32, kind="ExternalInput")
    d_b2 = nc.dram_tensor("b2", [NE, D], F32, kind="ExternalInput")
    d_w1 = nc.dram_tensor("w1", [NE, D, D2], F32, kind="ExternalInput")
    d_w2 = nc.dram_tensor("w2", [NE, D2, D], F32, kind="ExternalInput")
    d_eye = nc.dram_tensor("eye", [128, 128], BF16, kind="ExternalInput")
    d_out = nc.dram_tensor("out", [R, D], F32, kind="ExternalOutput")

    st = ExitStack()
    NB = 89632
    BB = st.enter_context(nc.sbuf_tensor("BB", [128, NB], BF16))
    NF = 7488
    FB = st.enter_context(nc.sbuf_tensor("FB", [128, NF], F32))

    QKV_O = 0
    XNFM_O = 0
    AFM_O = 16384
    XLO_O = 17408
    CWF_O = 18432
    PEO_O = 20480
    XSR_O = 28672
    OWT_O = XSR_O
    ATW_O = 36864
    XNF_O = 53248
    ACC_O = 69632
    K_OS = (XNF_O, XNF_O + 2048)
    Q_OS = (XNF_O + 4096, XNF_O + 6144)
    PT_O = XNF_O + 8192
    V_OS = (XNF_O + 12288, ACC_O + 9216)
    W1_O = XNF_O
    XNR_O = ACC_O
    XSB_O = ACC_O + 6144
    XNT_O = ACC_O + 8192
    OSB_O = ACC_O + 11296
    OSB32_O = ACC_O + 13856
    XHI_O = ACC_O + 11808
    XLOT_O = ACC_O + 12832
    assert XLOT_O + 1024 <= 86016
    CST_O = 86016
    EYE_O = CST_O
    ONE_O = CST_O + 128
    WRH_O = CST_O + 256
    WRL_O = CST_O + 320
    B2_O = CST_O + 384
    B1C_O = CST_O + 1408
    B1N_O = CST_O + 1536
    BRB_O = CST_O + 3584
    CWB_O = CST_O + 3592
    assert CST_O + 3600 <= NB

    XT_O = 0
    XS0_O = 3072
    XNF32_O = 5120
    LNS_O = 6144
    RTR_O = 6208
    CWT_O = 6272
    TAP_O = 6400
    RL_O = 6424
    WRN2_O = 6432
    EYE32_O = WRN2_O
    EPSC_O = 7456
    assert EPSC_O + 1 <= NF

    def bb(off, w, p0=0, p1=128):
        return BB[p0:p1, off:off + w]

    def fb(off, w, p0=0, p1=128):
        return FB[p0:p1, off:off + w]

    def osb32(col0, w, p0=0, p1=128):
        return BB[p0:p1, OSB32_O + 2 * col0:OSB32_O + 2 * (col0 + w)].bitcast(F32)

    ps_st0 = st.enter_context(nc.psum_tensor("ps_st0", [128, 1024], F32))
    ps_st1 = st.enter_context(nc.psum_tensor("ps_st1", [128, 1024], F32))
    ps_o0 = st.enter_context(nc.psum_tensor("ps_o0", [128, 512], F32))
    ps_m0 = st.enter_context(nc.psum_tensor("ps_m0", [128, 512], F32))
    ps_m1 = st.enter_context(nc.psum_tensor("ps_m1", [128, 512], F32))
    ps_t0 = st.enter_context(nc.psum_tensor("ps_t0", [128, 256], BF16))

    sem_names = ["x", "w", "g", "te", "act", "dve", "out"]
    sems = {n: st.enter_context(nc.semaphore(n)) for n in sem_names}

    plan = Plan()
    pool_m = PsumPool([ps_m0, ps_m1])
    pool_tr = PsumPool([ps_t0])
    pool_y = PsumPool(["y0", "y1", "y2", "y3"])

    def ybase(key):
        i = int(key[1])
        return (ps_st0 if i < 2 else ps_st1), (i % 2) * 512

    def add(eng, fn, waits=(), inc=None, k=1):
        ws = [(s, v) for (s, v) in waits if v is not None and v > 0]
        return plan.add(eng, fn, ws, inc, k)

    pending = deque()

    def push(fn, tag=None):
        pending.append((tag, fn))

    def inject(n=1):
        for _ in range(n):
            if pending:
                pending.popleft()[1]()

    def flush():
        while pending:
            pending.popleft()[1]()

    def drain_tag(tag):
        while any(t == tag for t, _ in pending):
            pending.popleft()[1]()

    add("gpsimd", lambda e: e.dma_start(bb(EYE_O, 128), d_eye[:, :]), inc="w", k=16)
    add("gpsimd", lambda e: e.dma_start(
        bass.AP(BB, PEO_O, [[NB, 128], [1024, 8], [1, 1024]]),
        bass.AP(d_peT, 0, [[1024, 128], [1024 * 128, 8], [1, 1024]])), inc="w", k=16)
    mark_w_pe = plan.val("w")
    add("gpsimd", lambda e: e.dma_start(
        fb(TAP_O, 24),
        bass.AP(d_taps, 0, [[3, 128], [3 * 128, 8], [1, 3]]),
        allow_slow_non_contiguous=True), inc="w", k=16)
    add("gpsimd", lambda e: e.memset(bass.AP(BB, V_OS[0] + 64, [[NB, 128], [65, 32], [1, 1]]), 1.0),
        inc="g")
    add("gpsimd", lambda e: e.memset(bass.AP(BB, V_OS[1] + 64, [[NB, 128], [65, 32], [1, 1]]), 1.0),
        inc="g")
    add("gpsimd", lambda e: e.memset(bb(ONE_O, 128, 0, 1), 1.0), inc="g")
    add("gpsimd", lambda e: e.memset(fb(EPSC_O, 1), EPS), inc="g")
    mark_g_ones = plan.val("g")
    add("gpsimd", lambda e: e.dma_start(
        bass.AP(BB, QKV_O, [[NB, 128], [2560, 8], [1, 2560]]),
        bass.AP(d_qkvT, 0, [[2560, 128], [2560 * 128, 8], [1, 2560]])), inc="w", k=16)
    mark_w_qkv = plan.val("w")
    add("gpsimd", lambda e: e.dma_start(fb(WRN2_O, 1024, 0, 8), d_wrT[:, :]), inc="w", k=16)
    add("gpsimd", lambda e: e.dma_start(bb(BRB_O, 8, 0, 1), d_br[:, :]), inc="w", k=16)
    add("gpsimd", lambda e: e.dma_start(bb(B1N_O, 2048, 0, 8), d_b1[:, :]), inc="w", k=16)
    add("gpsimd", lambda e: e.dma_start(bb(B2_O, 1024, 0, 8), d_b2[:, :]), inc="w", k=16)
    mark_w_small = plan.val("w")

    WRHN_O = PT_O
    WRLN_O = PT_O + 1024
    add("vector", lambda e: e.tensor_copy(bb(WRHN_O, 1024, 0, 8), fb(WRN2_O, 1024, 0, 8)),
        waits=[("w", mark_w_small)])
    add("vector", lambda e: e.drain())
    add("vector", lambda e: e.tensor_tensor(bb(WRLN_O, 1024, 0, 8), fb(WRN2_O, 1024, 0, 8),
                                            bb(WRHN_O, 1024, 0, 8), OP.subtract), inc="dve")
    mark_dve_wr = plan.val("dve")
    mark_eye32 = add("scalar", lambda e: e.copy(fb(EYE32_O, 128), bb(EYE_O, 128)),
                     waits=[("dve", mark_dve_wr), ("w", mark_w_pe)], inc="act")
    for src_off, dst_off in ((WRHN_O, WRH_O), (WRLN_O, WRL_O)):
        for c in range(8):
            pt, w = pool_tr.take()
            mk = add("tensor", (lambda e, pt=pt, so=src_off, c=c: e.transpose(
                pt[0:128, 0:8], bb(so + c * 128, 128, 0, 8), bb(EYE_O, 8, 0, 8))),
                waits=w + [("dve", mark_dve_wr), ("w", mark_w_pe)], inc="te")
            dv = add("vector", (lambda e, pt=pt, do=dst_off, c=c: e.tensor_copy(
                bb(do + c * 8, 8), pt[0:128, 0:8])), waits=[("te", mk)], inc="dve")
            pool_tr.release(pt, "dve", dv)
    for d2c in range(16):
        pt, w = pool_tr.take()
        mk = add("tensor", (lambda e, pt=pt, d2c=d2c: e.transpose(
            pt[0:128, 0:8], bb(B1N_O + d2c * 128, 128, 0, 8), bb(EYE_O, 8, 0, 8))),
            waits=w + [("w", mark_w_small)], inc="te")
        dv = add("vector", (lambda e, pt=pt, d2c=d2c: e.tensor_copy(
            bb(B1C_O + d2c * 8, 8), pt[0:128, 0:8])), waits=[("te", mk)], inc="dve")
        pool_tr.release(pt, "dve", dv)

    mark_x = {}
    mark_act_apply = {}
    mark_xn_evac = {}
    mark_xs = {}
    mark_pe_mm = {}
    mark_cast = {}
    mark_qkv_mm = {}
    mark_kqv = {}
    mark_qrep = {}
    mark_xnt_tr = {}
    mark_attn_done = {}
    mark_router_apply = {}
    mark_cw = {}
    mark_xnfm = {}
    mark_rtr_mex = {}

    def push_p1_tile(g, t):
        def u_load():
            slot = t % 3
            prev = t - 3
            waits = [mark_act_apply[prev]] if prev >= 0 else []
            mark_x[t] = add("sync", (lambda e: e.dma_start(
                fb(XT_O + slot * 1024, 1024), d_x[t * 128:(t + 1) * 128, :])),
                waits=waits, inc="x", k=16)
        push(u_load)

        def u_ln():
            slot = t % 3
            b = t % 2
            add("vector", (lambda e: e.bn_stats(
                fb(LNS_O + b * 16, 6), fb(XT_O + slot * 1024, 512))),
                waits=[("x", mark_x[t])] + ([mark_act_apply[t - 2]] if t >= 2 else []))
            add("vector", (lambda e: e.bn_stats(
                fb(LNS_O + b * 16 + 6, 6), fb(XT_O + slot * 1024 + 512, 512))))
            add("vector", lambda e: e.drain())
            mdve = add("vector", (lambda e: e.bn_aggr(
                fb(LNS_O + 32 + b * 2, 2), fb(LNS_O + b * 16, 12))), inc="dve")
            add("scalar", (lambda e: e.activation(
                fb(LNS_O + 36 + b * 3, 1), fb(LNS_O + 32 + b * 2 + 1, 1),
                AF.Ln, bias=fb(EPSC_O, 1), scale=1.0)),
                waits=[("dve", mdve), ("g", mark_g_ones)])
            add("scalar", lambda e: e.drain())
            mact = add("scalar", (lambda e: e.activation(
                fb(LNS_O + 37 + b * 3, 1), fb(LNS_O + 36 + b * 3, 1),
                AF.Exp, bias=0.0, scale=-0.5)), inc="act")
            add("vector", (lambda e: e.scalar_tensor_tensor(
                fb(LNS_O + 38 + b * 3, 1), fb(LNS_O + 32 + b * 2, 1), -1.0,
                fb(LNS_O + 37 + b * 3, 1), OP.mult, OP.mult)),
                waits=[("act", mact)], inc="dve")
            add("vector", lambda e: e.drain())
            mark_act_apply[t] = ("dve", add("vector", (lambda e: e.tensor_scalar(
                bb(XNT_O, 1024), fb(XT_O + slot * 1024, 1024),
                fb(LNS_O + 37 + b * 3, 1), fb(LNS_O + 38 + b * 3, 1),
                OP.mult, OP.add)),
                waits=[("te", mark_xnt_tr.get(t - 1))], inc="dve"))
        push(u_ln)

        for cc in range(8):
            def u_tr(c=cc):
                s3 = g % 3
                pt, w = pool_tr.take()
                mk = add("tensor", (lambda e, pt=pt, c=c: e.transpose(
                    pt[:, 0:128], bb(XNT_O + c * 128, 128), bb(EYE_O, 128))),
                    waits=w + [mark_act_apply[t]], inc="te")
                if c == 7:
                    mark_xnt_tr[t] = mk
                dv = add("vector", (lambda e, pt=pt, c=c: e.tensor_copy(
                    bb(XNR_O + s3 * 2048 + c * 256 + (t % 2) * 128, 128), pt[:, 0:128])),
                    waits=[("te", mk)], inc="dve")
                pool_tr.release(pt, "dve", dv)
                if c == 7 and t % 2 == 1:
                    mark_xn_evac[g] = plan.val("dve")
            push(u_tr)

    def push_p1_conv_pe(g):
        def u_conv():
            sA, sL, sR = g % 3, (g - 1) % 3, (g + 1) % 3
            waits = [("dve", mark_xn_evac[min(g + 1, 15)])]
            if g >= 1:
                waits.append(("te", mark_pe_mm.get(g - 1)))
                waits.append(mark_cast[g - 1])
            for c in range(8):
                add("vector", (lambda e, c=c: e.tensor_scalar(
                    fb(XS0_O + c * 256, 256), bb(XNR_O + sA * 2048 + c * 256, 256),
                    fb(TAP_O + c * 3 + 1, 1), None, OP.mult)),
                    waits=waits if c == 0 else [])
            for c in range(8):
                add("vector", (lambda e, c=c: e.scalar_tensor_tensor(
                    fb(XS0_O + c * 256 + 1, 255), bb(XNR_O + sA * 2048 + c * 256, 255),
                    fb(TAP_O + c * 3 + 0, 1), fb(XS0_O + c * 256 + 1, 255), OP.mult, OP.add)))
            if g > 0:
                for c in range(8):
                    add("vector", (lambda e, c=c: e.scalar_tensor_tensor(
                        fb(XS0_O + c * 256, 1), bb(XNR_O + sL * 2048 + c * 256 + 255, 1),
                        fb(TAP_O + c * 3 + 0, 1), fb(XS0_O + c * 256, 1), OP.mult, OP.add)))
            for c in range(8):
                add("vector", (lambda e, c=c: e.scalar_tensor_tensor(
                    fb(XS0_O + c * 256, 255), bb(XNR_O + sA * 2048 + c * 256 + 1, 255),
                    fb(TAP_O + c * 3 + 2, 1), fb(XS0_O + c * 256, 255), OP.mult, OP.add)))
            mconv = None
            if g < 15:
                for c in range(8):
                    mconv = add("vector", (lambda e, c=c: e.scalar_tensor_tensor(
                        fb(XS0_O + c * 256 + 255, 1), bb(XNR_O + sR * 2048 + c * 256, 1),
                        fb(TAP_O + c * 3 + 2, 1), fb(XS0_O + c * 256 + 255, 1),
                        OP.mult, OP.add)), inc="dve")
            else:
                mconv = add("vector", lambda e: e.engine_nop(), inc="dve")
            add("vector", lambda e: e.drain())
            mark_cast[g] = ("dve", add("vector", (lambda e: e.tensor_copy(
                bb(XSB_O, 2048), fb(XS0_O, 2048))),
                waits=[("te", mark_pe_mm.get(g - 1))], inc="dve"))
        push(u_conv)

        for o in range(8):
            def u_pe(o=o):
                slot4 = g % 4
                h_old = g - 4
                ps, w = pool_m.take()
                mk = None
                for ifc in range(8):
                    mk = add("tensor", (lambda e, ps=ps, ifc=ifc: e.matmul(
                        ps[:, 0:256], bb(PEO_O + ifc * 1024 + o * 128, 128),
                        bb(XSB_O + ifc * 256, 256), start=(ifc == 0), stop=(ifc == 7))),
                        waits=(w + [mark_cast[g], ("w", mark_w_pe)]) if ifc == 0 else [],
                        inc="te" if ifc == 7 else None)
                if o == 7:
                    mark_pe_mm[g] = mk
                ew = [("te", mk)]
                if h_old >= 0 and o == 0:
                    ew.append(("te", mark_qkv_mm.get(h_old)))
                dv = add("vector", (lambda e, ps=ps: e.tensor_add(
                    bb(XSR_O + slot4 * 2048 + o * 256, 256), ps[:, 0:256],
                    fb(XS0_O + o * 256, 256))), waits=ew, inc="dve")
                pool_m.release(ps, "dve", dv)
                if o == 7:
                    mark_xs[g] = plan.val("dve")
            push(u_pe)

    def push_p2_qkv(h):
        xb = XSR_O + (h % 4) * 2048
        ko, qo, vo = K_OS[h % 2], Q_OS[h % 2], V_OS[h % 2]
        for m in range(8):
            def u_k(m=m):
                ps, w = pool_m.take()
                mk = None
                for ifc in range(8):
                    mk = add("tensor", (lambda e, ps=ps, ifc=ifc, m=m: e.matmul(
                        ps[:, 0:256], bb(QKV_O + ifc * 2560 + 512 + m * 128, 128),
                        bb(xb + ifc * 256, 256), start=(ifc == 0), stop=(ifc == 7))),
                        waits=(w + [("dve", mark_xs[h]), ("w", mark_w_qkv)]) if ifc == 0 else [],
                        inc="te" if ifc == 7 else None)
                dv = add("vector", (lambda e, ps=ps, m=m: e.tensor_copy(
                    bb(ko + m * 256, 256), ps[:, 0:256])),
                    waits=[("te", mk), ("te", mark_attn_done.get(h - 2))], inc="dve")
                pool_m.release(ps, "dve", dv)
            push(u_k)
        for al in range(8):
            def u_q(al=al):
                ps, w = pool_m.take()
                mk = None
                for ifc in range(8):
                    mk = add("tensor", (lambda e, ps=ps, ifc=ifc: e.matmul(
                        ps[0:64, 0:256], bb(QKV_O + ifc * 2560 + al * 64, 64),
                        bb(xb + ifc * 256, 256), start=(ifc == 0), stop=(ifc == 7))),
                        waits=w if ifc == 0 else [], inc="te" if ifc == 7 else None)
                dv = add("vector", (lambda e, ps=ps: e.tensor_copy(
                    bb(qo + al * 256, 256, 0, 64), ps[0:64, 0:256])),
                    waits=[("te", mk), ("te", mark_attn_done.get(h - 2))], inc="dve")
                pool_m.release(ps, "dve", dv)
                if al == 7:
                    mark_qe = plan.val("dve")
                    mark_qrep[h] = add("gpsimd", (lambda e: e.dma_start(
                        bb(qo, 2048, 64, 128), bb(qo, 2048, 0, 64))),
                        waits=[("dve", mark_qe), ("te", mark_attn_done.get(h - 2))],
                        inc="w", k=16)
            push(u_q)
        for j in range(2):
            for vc in range(2):
                def u_v(j=j, vc=vc):
                    ps, w = pool_m.take()
                    mk = None
                    for ifc in range(8):
                        ww = []
                        if ifc == 0:
                            ww = w + ([("g", mark_g_ones)] if (h <= 1) else [])
                        mk = add("tensor", (lambda e, ps=ps, ifc=ifc: e.matmul(
                            ps[:, 0:512], bb(xb + ifc * 256 + j * 128, 128),
                            bb(QKV_O + ifc * 2560 + 1536 + vc * 512, 512),
                            start=(ifc == 0), stop=(ifc == 7))),
                            waits=ww, inc="te" if ifc == 7 else None)
                    mark_qkv_mm[h] = mk
                    dv = add("vector", (lambda e, ps=ps: e.tensor_copy(
                        bass.AP(BB, vo + j * 1040 + vc * 520, [[NB, 128], [65, 8], [1, 64]]),
                        ps[:, 0:512])),
                        waits=[("te", mk), ("te", mark_attn_done.get(h - 2))], inc="dve")
                    pool_m.release(ps, "dve", dv)
                    if j == 1 and vc == 1:
                        mark_kqv[h] = plan.val("dve")
                push(u_v)

    mark_o_free = [None, None]
    mark_st_free = [None]

    st_exp_mark = [None, None]
    av_hist = {}
    exp_hist = {}
    gpair = [0]
    mark_o_tr = [None]

    def p2_attn(h):
        ko, qo, vo = K_OS[h % 2], Q_OS[h % 2], V_OS[h % 2]
        for sqc in range(4):
            ob = 0
            ps_o = ps_o0
            base = gpair[0]
            for pp in range(16):
                k = base + pp
                pst = ps_st0 if k % 2 == 0 else ps_st1
                stw = [("act", st_exp_mark[k % 2])] if st_exp_mark[k % 2] else []
                if pp == 0:
                    stw.append(("dve", mark_kqv[h]))
                    stw.append(("w", mark_qrep[h]))
                m2 = pp // 2
                jj = pp % 2
                add("tensor", (lambda e, m2=m2, jj=jj, sqc=sqc, ko=ko, qo=qo,
                     pst=pst: e.matmul(
                    pst[:, 0:512], bb(ko + m2 * 256 + jj * 128, 128, 0, 64),
                    bb(qo + sqc * 512, 512, 0, 64), start=True, stop=True,
                    tile_position=(0, 0))),
                    waits=stw)
                tep = add("tensor", (lambda e, m2=m2, jj=jj, sqc=sqc, ko=ko, qo=qo,
                          pst=pst: e.matmul(
                    pst[:, 512:1024], bb(ko + m2 * 256 + jj * 128, 128, 64, 128),
                    bb(qo + sqc * 512, 512, 64, 128), start=True, stop=True,
                    tile_position=(64, 0))), inc="te")
                ew = [("te", tep)]
                if (k - 4) in av_hist:
                    ew.append(("te", av_hist[k - 4]))
                exp_hist[k] = add("scalar", (lambda e, k=k, pst=pst: e.activation(
                    bb(PT_O + (k % 4) * 1024, 1024), pst[:, 0:1024], AF.Exp,
                    bias=0.0, scale=0.125)), waits=ew, inc="act")
                st_exp_mark[k % 2] = exp_hist[k]
                mark_st_free[0] = exp_hist[k]
                inject(1)
                if pp >= 2:
                    q = pp - 2
                    kq = base + q
                    qm, qj = q // 2, q % 2
                    aw = [("act", exp_hist[kq])]
                    if q == 0 and mark_o_free[ob]:
                        aw.append(("dve", mark_o_free[ob]))
                    add("tensor", (lambda e, qm=qm, qj=qj, q=q, kq=kq, vo=vo,
                         ps_o=ps_o: e.matmul(
                        ps_o[0:65, 0:512], bb(vo + qj * 1040 + (2 * qm) * 65, 65),
                        bb(PT_O + (kq % 4) * 1024, 512), start=(q == 0), stop=False)),
                        waits=aw)
                    av_hist[kq] = add("tensor", (lambda e, qm=qm, qj=qj, kq=kq,
                                      vo=vo, ps_o=ps_o: e.matmul(
                        ps_o[0:65, 0:512], bb(vo + qj * 1040 + (2 * qm + 1) * 65, 65),
                        bb(PT_O + (kq % 4) * 1024 + 512, 512), start=False, stop=False)),
                        inc="te")
                inject(1)
            for q in (14, 15):
                kq = base + q
                qm, qj = q // 2, q % 2
                add("tensor", (lambda e, qm=qm, qj=qj, kq=kq, vo=vo, ps_o=ps_o: e.matmul(
                    ps_o[0:65, 0:512], bb(vo + qj * 1040 + (2 * qm) * 65, 65),
                    bb(PT_O + (kq % 4) * 1024, 512), start=False, stop=False)),
                    waits=[("act", exp_hist[kq])])
                av_hist[kq] = add("tensor", (lambda e, qm=qm, qj=qj, q=q, kq=kq,
                                  vo=vo, ps_o=ps_o: e.matmul(
                    ps_o[0:65, 0:512], bb(vo + qj * 1040 + (2 * qm + 1) * 65, 65),
                    bb(PT_O + (kq % 4) * 1024 + 512, 512), start=False,
                    stop=(q == 15))), inc="te")
            gpair[0] = base + 16
            drain_tag("ofin")
            dvo = add("vector", (lambda e, ps_o=ps_o: e.tensor_copy(
                osb32(0, 512, 0, 65), ps_o[0:65, 0:512])),
                waits=[("te", av_hist[base + 15]), ("te", mark_o_tr[0])], inc="dve")
            mark_o_free[ob] = dvo
            for q128 in range(4):
                def u_ofin(q128=q128, sqc=sqc, h=h, dvo=dvo):
                    pt, w = pool_m.take()
                    ti = sqc * 4 + q128
                    mk = add("tensor", (lambda e, pt=pt, q128=q128: e.transpose(
                        pt[:, 0:65], osb32(q128 * 128, 128, 0, 65),
                        fb(EYE32_O, 65, 0, 65))),
                        waits=w + [("dve", dvo), ("act", mark_eye32)], inc="te")
                    mark_o_tr[0] = mk
                    if q128 == 3:
                        mark_attn_done[h] = mk
                    add("vector", (lambda e, pt=pt, q128=q128: e.reciprocal(
                        fb(RL_O + q128 % 2, 1), pt[:, 64:65])), waits=[("te", mk)])
                    add("vector", lambda e: e.drain())
                    dv = add("vector", (lambda e, pt=pt, ti=ti, h=h, q128=q128: e.tensor_scalar(
                        bb(ATW_O + ti * 1024 + h * 64, 64), pt[:, 0:64],
                        fb(RL_O + q128 % 2, 1), None, OP.mult)), inc="dve")
                    pool_m.release(pt, "dve", dv)
                push(u_ofin, tag="ofin")

    def push_router(ti):
        def u_ld():
            slot = ti % 3
            waits = [mark_act_apply[31]]
            if ti >= 3:
                waits = [mark_router_apply[ti - 3]]
            mark_x[("r", ti)] = add("sync", (lambda e: e.dma_start(
                fb(XT_O + slot * 1024, 1024), d_xr[ti * 128:(ti + 1) * 128, :])),
                waits=waits, inc="x", k=16)
        push(u_ld)

        def u_ln():
            slot = ti % 3
            b = ti % 2
            add("vector", (lambda e: e.bn_stats(
                fb(LNS_O + b * 16, 6), fb(XT_O + slot * 1024, 512))),
                waits=[("x", mark_x[("r", ti)]),
                       (mark_router_apply[ti - 2] if ti >= 2
                        else mark_act_apply[31])])
            add("vector", (lambda e: e.bn_stats(
                fb(LNS_O + b * 16 + 6, 6), fb(XT_O + slot * 1024 + 512, 512))))
            add("vector", lambda e: e.drain())
            mdve = add("vector", (lambda e: e.bn_aggr(
                fb(LNS_O + 32 + b * 2, 2), fb(LNS_O + b * 16, 12))), inc="dve")
            add("scalar", (lambda e: e.activation(
                fb(LNS_O + 36 + b * 3, 1), fb(LNS_O + 32 + b * 2 + 1, 1), AF.Ln,
                bias=fb(EPSC_O, 1), scale=1.0)), waits=[("dve", mdve)])
            add("scalar", lambda e: e.drain())
            mact = add("scalar", (lambda e: e.activation(
                fb(LNS_O + 37 + b * 3, 1), fb(LNS_O + 36 + b * 3, 1),
                AF.Exp, bias=0.0, scale=-0.5)), inc="act")
            add("vector", (lambda e: e.scalar_tensor_tensor(
                fb(LNS_O + 38 + b * 3, 1), fb(LNS_O + 32 + b * 2, 1), -1.0,
                fb(LNS_O + 37 + b * 3, 1), OP.mult, OP.mult)),
                waits=[("act", mact)], inc="dve")
            add("vector", lambda e: e.drain())
            mark_router_apply[ti] = ("dve", add("vector", (lambda e: e.tensor_scalar(
                fb(XNF32_O, 1024), fb(XT_O + slot * 1024, 1024),
                fb(LNS_O + 37 + b * 3, 1), fb(LNS_O + 38 + b * 3, 1),
                OP.mult, OP.add)),
                waits=[("dve", mark_xnfm.get(ti - 1))], inc="dve"))
            add("vector", lambda e: e.drain())
            add("vector", (lambda e: e.tensor_copy(bb(XHI_O, 1024), fb(XNF32_O, 1024))),
                waits=[("te", mark_attn_done.get(15))])
            add("vector", lambda e: e.drain())
            add("vector", (lambda e: e.tensor_tensor(
                bb(XLOT_O, 1024), fb(XNF32_O, 1024), bb(XHI_O, 1024), OP.subtract)),
                inc="dve")
        push(u_ln)

        for cc in range(8):
            def u_tr(c=cc):
                mhl = plan.val("dve")
                pt, w = pool_tr.take()
                mk = add("tensor", (lambda e, pt=pt, c=c: e.transpose(
                    pt[:, 0:128], bb(XHI_O + c * 128, 128), bb(EYE_O, 128))),
                    waits=w + [("dve", mhl), ("te", mark_qkv_mm[15])], inc="te")
                dv = add("vector", (lambda e, pt=pt, c=c: e.tensor_copy(
                    bb(XNFM_O + c * 2048 + ti * 128, 128), pt[:, 0:128])),
                    waits=[("te", mk)], inc="dve")
                pool_tr.release(pt, "dve", dv)
                pt, w = pool_tr.take()
                mk = add("tensor", (lambda e, pt=pt, c=c: e.transpose(
                    pt[:, 0:128], bb(XLOT_O + c * 128, 128), bb(EYE_O, 128))),
                    waits=w, inc="te")
                dv = add("vector", (lambda e, pt=pt, c=c: e.tensor_copy(
                    bb(XLO_O + c * 128, 128), pt[:, 0:128])), waits=[("te", mk)], inc="dve")
                pool_tr.release(pt, "dve", dv)
                if c == 7:
                    mark_xnfm[ti] = dv
            push(u_tr)

        def u_mm():
            mtrans = mark_xnfm[ti]
            ps, w = pool_m.take()
            first = True
            for (lo_lhs, rhs_off) in ((0, WRH_O), (0, WRL_O), (1, WRH_O)):
                for ifc in range(8):
                    lhs_off = (XLO_O + ifc * 128) if lo_lhs else (XNFM_O + ifc * 2048 + ti * 128)
                    add("tensor", (lambda e, ps=ps, lo=lhs_off, ro=rhs_off, ifc=ifc,
                                   fr=first: e.matmul(
                        ps[:, 0:8], bb(lo, 128), bb(ro + ifc * 8, 8),
                        start=fr, stop=False)),
                        waits=(w + [("dve", mtrans)]) if first else [])
                    first = False
            mk = add("tensor", (lambda e, ps=ps: e.matmul(
                ps[:, 0:8], bb(ONE_O, 128, 0, 1), bb(BRB_O, 8, 0, 1),
                start=False, stop=True)), inc="te")
            mnm = add("vector", (lambda e, ps=ps: e.reduce_max(
                fb(RTR_O, 1), ps[:, 0:8], axis=AX, negate=True)),
                waits=[("te", mk), ("act", mark_rtr_mex.get(ti - 1))], inc="dve")
            mex = add("scalar", (lambda e, ps=ps: e.activation(
                fb(RTR_O + 8, 8), ps[:, 0:8], AF.Exp, bias=fb(RTR_O, 1), scale=1.0)),
                waits=[("dve", mnm)], inc="act")
            mark_rtr_mex[ti] = mex
            pool_m.release(ps, "act", mex)
            add("vector", (lambda e: e.reduce_sum(fb(RTR_O + 1, 1), fb(RTR_O + 8, 8), axis=AX)),
                waits=[("act", mex)])
            add("vector", lambda e: e.drain())
            add("vector", (lambda e: e.reciprocal(fb(RTR_O + 2, 1), fb(RTR_O + 1, 1))))
            add("vector", lambda e: e.drain())
            add("vector", (lambda e: e.tensor_scalar(
                fb(RTR_O + 16, 8), fb(RTR_O + 8, 8), fb(RTR_O + 2, 1), None, OP.mult)))
            add("vector", lambda e: e.drain())
            add("vector", (lambda e: e.reduce_max(fb(RTR_O + 3, 1), fb(RTR_O + 16, 8), axis=AX)))
            add("vector", lambda e: e.drain())
            add("vector", (lambda e: e.tensor_scalar(
                fb(RTR_O + 24, 8), fb(RTR_O + 16, 8), fb(RTR_O + 3, 1), None, OP.is_ge)))
            add("vector", lambda e: e.drain())
            add("vector", (lambda e: e.scalar_tensor_tensor(
                fb(RTR_O + 32, 8), fb(RTR_O + 24, 8), -1e30, fb(RTR_O + 16, 8),
                OP.mult, OP.add)))
            add("vector", lambda e: e.drain())
            add("vector", (lambda e: e.reduce_max(fb(RTR_O + 4, 1), fb(RTR_O + 32, 8), axis=AX)))
            add("vector", lambda e: e.drain())
            add("vector", (lambda e: e.tensor_scalar(
                fb(RTR_O + 40, 8), fb(RTR_O + 16, 8), fb(RTR_O + 4, 1), None, OP.is_ge)))
            add("vector", lambda e: e.drain())
            add("vector", (lambda e: e.tensor_mul(
                fb(CWT_O + ti * 8, 8), fb(RTR_O + 16, 8), fb(RTR_O + 40, 8))))
            add("vector", lambda e: e.drain())
            mcwb = add("vector", (lambda e: e.tensor_copy(
                bb(CWB_O, 8), fb(CWT_O + ti * 8, 8))), inc="dve")
            pt, w = pool_tr.take()
            mk = add("tensor", (lambda e, pt=pt: e.transpose(
                pt[0:8, 0:128], bb(CWB_O, 8), bb(EYE_O, 128))),
                waits=w + [("dve", mcwb)], inc="te")
            dv = add("vector", (lambda e, pt=pt: e.tensor_copy(
                bb(CWF_O + ti * 128, 128, 0, 8), pt[0:8, 0:128])),
                waits=[("te", mk)], inc="dve")
            pool_tr.release(pt, "dve", dv)
            mark_cw[ti] = plan.val("dve")
        push(u_mm)

    for it in range(18):
        if it < 16:
            push_p1_tile(it, 2 * it)
            push_p1_tile(it, 2 * it + 1)
        if 1 <= it <= 16:
            push_p1_conv_pe(it - 1)
        if 0 <= it - 1 < 16:
            push_p2_qkv(it - 1)
        if it == 17:
            for ti in range(16):
                push_router(ti)
        hh = it - 2
        if 0 <= hh < 16:
            p2_attn(hh)
        flush()

    mark_attn_all = plan.val("dve")

    add("gpsimd", lambda e: e.dma_start(
        bass.AP(BB, OWT_O, [[NB, 128], [1024, 8], [1, 1024]]),
        bass.AP(d_owT, 0, [[1024, 128], [1024 * 128, 8], [1, 1024]])),
        waits=[("te", mark_qkv_mm[15])], inc="w", k=16)
    mark_w_ow = plan.val("w")

    for it_ in pool_y.items:
        it_[1] = ("act", mark_st_free[0])

    mark_p3_te_last = None
    mark_acc_last = {}
    mark_p3_trans_last = None
    for ti in range(16):
        dvs = []
        for c in range(8):
            pt, w = pool_tr.take()
            mk = add("tensor", (lambda e, pt=pt, ti=ti, c=c: e.transpose(
                pt[:, 0:128], bb(ATW_O + ti * 1024 + c * 128, 128), bb(EYE_O, 128))),
                waits=w + ([("dve", mark_attn_all)] if (ti == 0 and c == 0) else []), inc="te")
            mark_p3_trans_last = mk
            dv = add("vector", (lambda e, pt=pt, c=c: e.tensor_copy(
                bb(AFM_O + c * 128, 128), pt[:, 0:128])), waits=[("te", mk)], inc="dve")
            pool_tr.release(pt, "dve", dv)
            dvs.append(dv)
        for half in range(2):
            key, w = pool_y.take()
            yt, yb = ybase(key)
            mk = None
            for c in range(8):
                mk = add("tensor", (lambda e, yt=yt, yb=yb, c=c, half=half: e.matmul(
                    yt[:, yb:yb + 512], bb(AFM_O + c * 128, 128),
                    bb(OWT_O + c * 1024 + half * 512, 512), start=(c == 0), stop=False)),
                    waits=(w + [("dve", dvs[-1]), ("w", mark_w_ow), ("dve", mark_cw[ti])])
                    if c == 0 else [])
            mk = add("tensor", (lambda e, yt=yt, yb=yb, ti=ti, half=half: e.matmul(
                yt[:, yb:yb + 512], bb(CWF_O + ti * 128, 128, 0, 8),
                bb(B2_O + half * 512, 512, 0, 8), start=False, stop=True)), inc="te")
            mark_p3_te_last = mk
            dv = add("vector", (lambda e, yt=yt, yb=yb, ti=ti, half=half: e.tensor_copy(
                bb(ACC_O + ti * 1024 + half * 512, 512), yt[:, yb:yb + 512])),
                waits=[("te", mk)], inc="dve")
            pool_y.release(key, "dve", dv)
            mark_acc_last[(ti, half)] = dv

    mark_w1_mm_done = None
    mark_w2_mm_done = mark_p3_te_last
    mark_w1_load = None
    mark_w2_load = None
    first_w2 = True
    for ee in range(8):
        for s in [0]:
            w1_waits = [("te", mark_w1_mm_done)] if mark_w1_mm_done is not None else \
                [("te", mark_attn_done[15])]
            mark_w1_load = add("gpsimd", (lambda e, ee=ee: e.dma_start(
                bass.AP(BB, W1_O, [[NB, 128], [2048, 8], [1, 2048]]),
                bass.AP(d_w1, ee * D * D2, [[2048, 128], [2048 * 128, 8], [1, 2048]]))),
                waits=w1_waits, inc="w", k=16)
            w2_waits = [("te", mark_p3_trans_last if first_w2 else mark_w2_mm_done)]
            first_w2 = False
            mark_w2_load = add("gpsimd", (lambda e, ee=ee: e.dma_start(
                bass.AP(BB, ATW_O, [[NB, 128], [1024, 16], [1, 1024]]),
                bass.AP(d_w2, ee * D2 * D, [[1024, 128], [1024 * 128, 16], [1, 1024]]))),
                waits=w2_waits, inc="w", k=16)
            for tc in range(4):
                hmarks = []
                for d2c in range(16):
                    ps, w = pool_m.take()
                    mk = None
                    for ifc in range(8):
                        waits = []
                        if ifc == 0:
                            waits = w + [("w", mark_w1_load), ("dve", mark_xnfm.get(15))]
                        mk = add("tensor", (lambda e, ps=ps, d2c=d2c, ifc=ifc, s=s, tc=tc: e.matmul(
                            ps[:, 0:512], bb(W1_O + ifc * 2048 + d2c * 128, 128),
                            bb(XNFM_O + ifc * 2048 + tc * 512, 512),
                            start=(ifc == 0), stop=(ifc == 7))),
                            waits=waits, inc="te" if ifc == 7 else None)
                    mark_w1_mm_done = mk
                    hw = [("te", mk)]
                    if mark_w2_mm_done is not None:
                        hw.append(("te", mark_w2_mm_done))
                    hm = add("scalar", (lambda e, ps=ps, d2c=d2c, ee=ee: e.activation(
                        bb(PEO_O + d2c * 512, 512), ps[:, 0:512], AF.Relu,
                        bias=bb(B1C_O + d2c * 8 + ee, 1))), waits=hw, inc="act")
                    pool_m.release(ps, "act", hm)
                    hmarks.append(hm)
                for tj in range(4):
                    ti = tc * 4 + tj
                    for half in range(2):
                        key, w = pool_y.take()
                        yt, yb = ybase(key)
                        mk = None
                        for d2c in range(16):
                            waits = [("act", hmarks[d2c])]
                            if d2c == 0:
                                waits += w + [("w", mark_w2_load)]
                            mk = add("tensor", (lambda e, yt=yt, yb=yb, d2c=d2c, tj=tj,
                                     half=half: e.matmul(
                                yt[:, yb:yb + 512],
                                bb(PEO_O + d2c * 512 + tj * 128, 128),
                                bb(ATW_O + d2c * 1024 + half * 512, 512),
                                start=(d2c == 0), stop=(d2c == 15))),
                                waits=waits, inc="te" if d2c == 15 else None)
                        mark_w2_mm_done = mk
                        dv = add("vector", (lambda e, yt=yt, yb=yb, ti=ti, half=half, ee=ee:
                                 e.scalar_tensor_tensor(
                            bb(ACC_O + ti * 1024 + half * 512, 512),
                            yt[:, yb:yb + 512],
                            fb(CWT_O + ti * 8 + ee, 1), bb(ACC_O + ti * 1024 + half * 512, 512),
                            OP.mult, OP.add)),
                            waits=[("te", mk), ("dve", mark_acc_last[(ti, half)])], inc="dve")
                        mark_acc_last[(ti, half)] = dv
                        pool_y.release(key, "dve", dv)

    mark_out_stage = {}
    for ti in range(16):
        slot = ti % 3
        waits = [mark_router_apply[15]]
        if ti >= 3:
            waits = [("dve", mark_out_stage[ti - 3])]
        mx = add("sync", (lambda e, ti=ti, slot=slot: e.dma_start(
            fb(XT_O + slot * 1024, 1024), d_xr[ti * 128:(ti + 1) * 128, :])),
            waits=waits, inc="x", k=16)
        ob = ti % 2
        dv = add("vector", (lambda e, slot=slot, ob=ob, ti=ti: e.tensor_tensor(
            fb(XS0_O + ob * 1024, 1024), fb(XT_O + slot * 1024, 1024),
            bb(ACC_O + ti * 1024, 1024), OP.add)),
            waits=[("x", mx), ("dve", mark_acc_last[(ti, 0)]), ("dve", mark_acc_last[(ti, 1)]),
                   ("out", None if ti < 2 else (ti - 1) * 16)],
            inc="dve")
        mark_out_stage[ti] = dv
        add("sync", (lambda e, ti=ti, ob=ob: e.dma_start(
            d_out[ti * 128:(ti + 1) * 128, :], fb(XS0_O + ob * 1024, 1024))),
            waits=[("dve", dv)], inc="out", k=16)
    add("sync", lambda e: e.nop(), waits=[("out", plan.val("out"))])

    simulate(plan)

    with st, nc.Block() as block:
        def replay(name):
            def f(eng):
                hw = {}
                for waits, fn, inc, k in plan.q[name]:
                    for sname, v in waits:
                        if hw.get(sname, 0) < v:
                            eng.wait_ge(sems[sname], v)
                            hw[sname] = v
                    inst = fn(eng)
                    if inc is not None:
                        inst.then_inc(sems[inc], k)
            return f

        block.tensor(replay("tensor"))
        block.vector(replay("vector"))
        block.scalar(replay("scalar"))
        block.gpsimd(replay("gpsimd"))
        block.sync(replay("sync"))

    return nc


_CACHE = {}


def kernel(L0_attn_x, L0_attn_conv_w, L0_attn_pe_w, L0_attn_qkv_w, L0_attn_out_w,
           L0_ffn_Wr, L0_ffn_br, L0_ffn_W1, L0_ffn_b1, L0_ffn_W2, L0_ffn_b2):
    x = np.ascontiguousarray(np.asarray(L0_attn_x, dtype=np.float32))
    qkv_w = np.asarray(L0_attn_qkv_w, dtype=np.float32)
    pe_wT = np.ascontiguousarray(np.asarray(L0_attn_pe_w, dtype=np.float32).T)
    out_wT = np.ascontiguousarray(np.asarray(L0_attn_out_w, dtype=np.float32).T)
    taps = np.ascontiguousarray(np.asarray(L0_attn_conv_w, dtype=np.float32)[:, 0, :, 1])
    wrT = np.ascontiguousarray(np.asarray(L0_ffn_Wr, dtype=np.float32).T)
    br = np.ascontiguousarray(np.asarray(L0_ffn_br, dtype=np.float32).reshape(1, NE))
    b1 = np.ascontiguousarray(np.asarray(L0_ffn_b1, dtype=np.float32))
    b2 = np.ascontiguousarray(np.asarray(L0_ffn_b2, dtype=np.float32))
    w1 = np.ascontiguousarray(np.asarray(L0_ffn_W1, dtype=np.float32))
    w2 = np.ascontiguousarray(np.asarray(L0_ffn_W2, dtype=np.float32))
    eye = np.eye(128, dtype=ml_dtypes.bfloat16)

    if "nc" not in _CACHE:
        _CACHE["nc"] = build_nc()
    nc = _CACHE["nc"]

    o = np.arange(R)
    in_maps = []
    idxs = []
    for c in range(8):
        b, p = c // 2, c % 2
        idx = 16 * (o % 256) + 8 * p + o // 256
        idxs.append((b, idx))
        qkvT_local = np.ascontiguousarray(
            np.concatenate([qkv_w[512 * p:512 * (p + 1)], qkv_w[1024:3072]], 0).T)
        in_maps.append({
            "x_full": np.ascontiguousarray(x[b]),
            "x_res": np.ascontiguousarray(x[b][idx]),
            "qkvT": qkvT_local,
            "peT": pe_wT, "owT": out_wT, "taps": taps, "wrT": wrT, "br": br,
            "b1": b1, "b2": b2, "w1": w1, "w2": w2, "eye": eye,
        })

    res = run_bass_kernel_spmd(nc, in_maps, core_ids=list(range(8)))
    out = np.empty((4, S, D), dtype=np.float32)
    for c in range(8):
        b, idx = idxs[c]
        out[b][idx] = np.asarray(res.results[c]["out"], dtype=np.float32)
    return out


# version 19 (final)
# speedup vs baseline: 1.0193x; 1.0136x over previous
import numpy as np
import ml_dtypes
from collections import deque
from contextlib import ExitStack

import concourse.bass as bass
import concourse.mybir as mybir
from concourse.bass_utils import run_bass_kernel_spmd

F32 = mybir.dt.float32
BF16 = mybir.dt.bfloat16
AX = mybir.AxisListType.X
OP = mybir.AluOpType
AF = mybir.ActivationFunctionType

D = 1024
S = 4096
R = 2048
NE = 8
D2 = 2048
EPS = 1e-5

ENGINES = ("tensor", "vector", "scalar", "gpsimd", "sync")


class Plan:
    def __init__(self):
        self.q = {e: [] for e in ENGINES}
        self.cnt = {}

    def add(self, eng, fn, waits=(), inc=None, k=1):
        self.q[eng].append((tuple(waits), fn, inc, k))
        if inc is not None:
            self.cnt[inc] = self.cnt.get(inc, 0) + k
            return self.cnt[inc]
        return None

    def val(self, sem):
        return self.cnt.get(sem, 0)


def simulate(plan):
    idx = {e: 0 for e in ENGINES}
    sem = {}
    progress = True
    while progress:
        progress = False
        for e in ENGINES:
            while idx[e] < len(plan.q[e]):
                waits, _, inc, k = plan.q[e][idx[e]]
                if all(sem.get(s, 0) >= v for s, v in waits):
                    if inc is not None:
                        sem[inc] = sem.get(inc, 0) + k
                    idx[e] += 1
                    progress = True
                else:
                    break
    stuck = {e: (idx[e], len(plan.q[e])) for e in ENGINES if idx[e] < len(plan.q[e])}
    if stuck:
        det = {}
        for e in stuck:
            det[e] = [(s, v, sem.get(s, 0)) for s, v in plan.q[e][idx[e]][0]]
        raise RuntimeError(f"deadlock in plan: stuck={stuck} waits={det}")


class PsumPool:
    def __init__(self, tensors):
        self.items = [[t, None] for t in tensors]
        self.i = 0

    def take(self):
        it = self.items[self.i]
        self.i = (self.i + 1) % len(self.items)
        return it[0], ([it[1]] if it[1] is not None else [])

    def release(self, t, sem, val):
        for it in self.items:
            if it[0] is t:
                it[1] = (sem, val)
                return
        raise KeyError


def build_nc():
    nc = bass.Bass("TRN2", target_bir_lowering=False)

    d_x = nc.dram_tensor("x_full", [S, D], F32, kind="ExternalInput")
    d_xr = nc.dram_tensor("x_res", [R, D], F32, kind="ExternalInput")
    d_qkvT = nc.dram_tensor("qkvT", [D, 2560], F32, kind="ExternalInput")
    d_peT = nc.dram_tensor("peT", [D, D], F32, kind="ExternalInput")
    d_owT = nc.dram_tensor("owT", [D, D], F32, kind="ExternalInput")
    d_taps = nc.dram_tensor("taps", [D, 3], F32, kind="ExternalInput")
    d_wrT = nc.dram_tensor("wrT", [NE, D], F32, kind="ExternalInput")
    d_br = nc.dram_tensor("br", [1, NE], F32, kind="ExternalInput")
    d_b1 = nc.dram_tensor("b1", [NE, D2], F32, kind="ExternalInput")
    d_b2 = nc.dram_tensor("b2", [NE, D], F32, kind="ExternalInput")
    d_w1 = nc.dram_tensor("w1", [NE, D, D2], F32, kind="ExternalInput")
    d_w2 = nc.dram_tensor("w2", [NE, D2, D], F32, kind="ExternalInput")
    d_eye = nc.dram_tensor("eye", [128, 128], BF16, kind="ExternalInput")
    d_out = nc.dram_tensor("out", [R, D], F32, kind="ExternalOutput")

    st = ExitStack()
    NB = 89760
    BB = st.enter_context(nc.sbuf_tensor("BB", [128, NB], BF16))
    NF = 7488
    FB = st.enter_context(nc.sbuf_tensor("FB", [128, NF], F32))

    QKV_O = 0
    XNFM_O = 0
    AFM_O = 16384
    XLO_O = 17408
    CWF_O = 18432
    PEO_O = 20480
    XSR_O = 28672
    OWT_O = XSR_O
    ATW_O = 36864
    XNF_O = 53248
    ACC_O = 69632
    K_OS = (XNF_O, XNF_O + 2048)
    Q_OS = (XNF_O + 4096, XNF_O + 6144)
    PT_O = XNF_O + 8192
    V_OS = (XNF_O + 12288, ACC_O + 9216)
    W1_O = XNF_O
    XNR_O = ACC_O
    XSB_O = ACC_O + 6144
    XNT_O = ACC_O + 8192
    OSB_O = ACC_O + 11296
    OSB32_O = ACC_O + 13856
    XHI_O = ACC_O + 11808
    XLOT_O = ACC_O + 12832
    assert XLOT_O + 1024 <= 86016
    CST_O = 86016
    EYE_O = CST_O
    ONE_O = CST_O + 128
    WRH_O = CST_O + 256
    WRL_O = CST_O + 320
    B2_O = CST_O + 384
    B1C_O = CST_O + 1408
    B1N_O = CST_O + 1536
    BRB_O = CST_O + 3584
    CWB_O = CST_O + 3592
    assert CST_O + 3720 <= NB

    XT_O = 0
    XS0_O = 3072
    XNF32_O = 5120
    LNS_O = 6144
    RTR_O = 6208
    CWT_O = 6272
    TAP_O = 6400
    RL_O = 6424
    WRN2_O = 6432
    EYE32_O = WRN2_O
    LGT_O = WRN2_O + 128
    RSC_O = WRN2_O + 256
    EPSC_O = 7456
    assert EPSC_O + 1 <= NF

    def bb(off, w, p0=0, p1=128):
        return BB[p0:p1, off:off + w]

    def fb(off, w, p0=0, p1=128):
        return FB[p0:p1, off:off + w]

    def osb32(col0, w, p0=0, p1=128):
        return BB[p0:p1, OSB32_O + 2 * col0:OSB32_O + 2 * (col0 + w)].bitcast(F32)

    ps_st0 = st.enter_context(nc.psum_tensor("ps_st0", [128, 1024], F32))
    ps_st1 = st.enter_context(nc.psum_tensor("ps_st1", [128, 1024], F32))
    ps_o0 = st.enter_context(nc.psum_tensor("ps_o0", [128, 512], F32))
    ps_m0 = st.enter_context(nc.psum_tensor("ps_m0", [128, 512], F32))
    ps_m1 = st.enter_context(nc.psum_tensor("ps_m1", [128, 512], F32))
    ps_t0 = st.enter_context(nc.psum_tensor("ps_t0", [128, 256], BF16))

    sem_names = ["x", "w", "g", "te", "act", "dve", "out"]
    sems = {n: st.enter_context(nc.semaphore(n)) for n in sem_names}

    plan = Plan()
    pool_m = PsumPool([ps_m0, ps_m1])
    pool_tr = PsumPool([ps_t0])
    pool_y = PsumPool(["y0", "y1", "y2", "y3"])

    def ybase(key):
        i = int(key[1])
        return (ps_st0 if i < 2 else ps_st1), (i % 2) * 512

    def add(eng, fn, waits=(), inc=None, k=1):
        ws = [(s, v) for (s, v) in waits if v is not None and v > 0]
        return plan.add(eng, fn, ws, inc, k)

    pending = deque()

    def push(fn, tag=None):
        pending.append((tag, fn))

    def inject(n=1):
        for _ in range(n):
            if pending:
                pending.popleft()[1]()

    def flush():
        while pending:
            pending.popleft()[1]()

    def drain_tag(tag):
        while any(t == tag for t, _ in pending):
            pending.popleft()[1]()

    add("gpsimd", lambda e: e.dma_start(bb(EYE_O, 128), d_eye[:, :]), inc="w", k=16)
    add("gpsimd", lambda e: e.dma_start(
        bass.AP(BB, PEO_O, [[NB, 128], [1024, 8], [1, 1024]]),
        bass.AP(d_peT, 0, [[1024, 128], [1024 * 128, 8], [1, 1024]])), inc="w", k=16)
    mark_w_pe = plan.val("w")
    add("gpsimd", lambda e: e.dma_start(
        fb(TAP_O, 24),
        bass.AP(d_taps, 0, [[3, 128], [3 * 128, 8], [1, 3]]),
        allow_slow_non_contiguous=True), inc="w", k=16)
    add("gpsimd", lambda e: e.memset(bass.AP(BB, V_OS[0] + 64, [[NB, 128], [65, 32], [1, 1]]), 1.0),
        inc="g")
    add("gpsimd", lambda e: e.memset(bass.AP(BB, V_OS[1] + 64, [[NB, 128], [65, 32], [1, 1]]), 1.0),
        inc="g")
    add("gpsimd", lambda e: e.memset(bb(ONE_O, 128, 0, 1), 1.0), inc="g")
    add("gpsimd", lambda e: e.memset(fb(EPSC_O, 1), EPS), inc="g")
    mark_g_ones = plan.val("g")
    add("gpsimd", lambda e: e.dma_start(
        bass.AP(BB, QKV_O, [[NB, 128], [2560, 8], [1, 2560]]),
        bass.AP(d_qkvT, 0, [[2560, 128], [2560 * 128, 8], [1, 2560]])), inc="w", k=16)
    mark_w_qkv = plan.val("w")
    add("gpsimd", lambda e: e.dma_start(fb(WRN2_O, 1024, 0, 8), d_wrT[:, :]), inc="w", k=16)
    add("gpsimd", lambda e: e.dma_start(bb(BRB_O, 8, 0, 1), d_br[:, :]), inc="w", k=16)
    add("gpsimd", lambda e: e.dma_start(bb(B1N_O, 2048, 0, 8), d_b1[:, :]), inc="w", k=16)
    add("gpsimd", lambda e: e.dma_start(bb(B2_O, 1024, 0, 8), d_b2[:, :]), inc="w", k=16)
    mark_w_small = plan.val("w")

    WRHN_O = PT_O
    WRLN_O = PT_O + 1024
    add("vector", lambda e: e.tensor_copy(bb(WRHN_O, 1024, 0, 8), fb(WRN2_O, 1024, 0, 8)),
        waits=[("w", mark_w_small)])
    add("vector", lambda e: e.drain())
    add("vector", lambda e: e.tensor_tensor(bb(WRLN_O, 1024, 0, 8), fb(WRN2_O, 1024, 0, 8),
                                            bb(WRHN_O, 1024, 0, 8), OP.subtract), inc="dve")
    mark_dve_wr = plan.val("dve")
    mark_eye32 = add("scalar", lambda e: e.copy(fb(EYE32_O, 128), bb(EYE_O, 128)),
                     waits=[("dve", mark_dve_wr), ("w", mark_w_pe)], inc="act")
    for src_off, dst_off in ((WRHN_O, WRH_O), (WRLN_O, WRL_O)):
        for c in range(8):
            pt, w = pool_tr.take()
            mk = add("tensor", (lambda e, pt=pt, so=src_off, c=c: e.transpose(
                pt[0:128, 0:8], bb(so + c * 128, 128, 0, 8), bb(EYE_O, 8, 0, 8))),
                waits=w + [("dve", mark_dve_wr), ("w", mark_w_pe)], inc="te")
            dv = add("vector", (lambda e, pt=pt, do=dst_off, c=c: e.tensor_copy(
                bb(do + c * 8, 8), pt[0:128, 0:8])), waits=[("te", mk)], inc="dve")
            pool_tr.release(pt, "dve", dv)
    for d2c in range(16):
        pt, w = pool_tr.take()
        mk = add("tensor", (lambda e, pt=pt, d2c=d2c: e.transpose(
            pt[0:128, 0:8], bb(B1N_O + d2c * 128, 128, 0, 8), bb(EYE_O, 8, 0, 8))),
            waits=w + [("w", mark_w_small)], inc="te")
        dv = add("vector", (lambda e, pt=pt, d2c=d2c: e.tensor_copy(
            bb(B1C_O + d2c * 8, 8), pt[0:128, 0:8])), waits=[("te", mk)], inc="dve")
        pool_tr.release(pt, "dve", dv)

    mark_x = {}
    mark_act_apply = {}
    mark_xn_evac = {}
    mark_xs = {}
    mark_pe_mm = {}
    mark_cast = {}
    mark_qkv_mm = {}
    mark_kqv = {}
    mark_qrep = {}
    mark_xnt_tr = {}
    mark_attn_done = {}
    mark_router_apply = {}
    mark_cw = {}
    mark_xnfm = {}
    mark_rtr_mex = {}
    mark_logits = {}

    def push_p1_tile(g, t):
        def u_load():
            slot = t % 3
            prev = t - 3
            waits = [("act", mark_act_apply.get(prev))] if prev >= 0 else []
            mark_x[t] = add("sync", (lambda e: e.dma_start(
                fb(XT_O + slot * 1024, 1024), d_x[t * 128:(t + 1) * 128, :])),
                waits=waits, inc="x", k=16)
        push(u_load)

        def u_ln():
            slot = t % 3
            b = t % 2
            add("vector", (lambda e: e.bn_stats(
                fb(LNS_O + b * 16, 6), fb(XT_O + slot * 1024, 512))),
                waits=[("x", mark_x[t]), ("act", mark_act_apply.get(t - 2))])
            add("vector", (lambda e: e.bn_stats(
                fb(LNS_O + b * 16 + 6, 6), fb(XT_O + slot * 1024 + 512, 512))))
            add("vector", lambda e: e.drain())
            mdve = add("vector", (lambda e: e.bn_aggr(
                fb(LNS_O + 32 + b * 2, 2), fb(LNS_O + b * 16, 12))), inc="dve")
            add("scalar", (lambda e: e.activation(
                fb(LNS_O + 36 + b * 3, 1), fb(LNS_O + 32 + b * 2 + 1, 1),
                AF.Ln, bias=fb(EPSC_O, 1), scale=1.0)),
                waits=[("dve", mdve), ("g", mark_g_ones)])
            add("scalar", lambda e: e.drain())
            mact = add("scalar", (lambda e: e.activation(
                fb(LNS_O + 37 + b * 3, 1), fb(LNS_O + 36 + b * 3, 1),
                AF.Exp, bias=0.0, scale=-0.5)), inc="act")
            mdve = add("vector", (lambda e: e.scalar_tensor_tensor(
                fb(LNS_O + 38 + b * 3, 1), fb(LNS_O + 32 + b * 2, 1), -1.0,
                fb(LNS_O + 37 + b * 3, 1), OP.mult, OP.mult)),
                waits=[("act", mact)], inc="dve")
            mark_act_apply[t] = add("scalar", (lambda e: e.activation(
                bb(XNT_O, 1024), fb(XT_O + slot * 1024, 1024), AF.Identity,
                bias=fb(LNS_O + 38 + b * 3, 1), scale=fb(LNS_O + 37 + b * 3, 1))),
                waits=[("dve", mdve), ("te", mark_xnt_tr.get(t - 1))], inc="act")
        push(u_ln)

        for cc in range(8):
            def u_tr(c=cc):
                s3 = g % 3
                pt, w = pool_tr.take()
                mk = add("tensor", (lambda e, pt=pt, c=c: e.transpose(
                    pt[:, 0:128], bb(XNT_O + c * 128, 128), bb(EYE_O, 128))),
                    waits=w + [("act", mark_act_apply[t])], inc="te")
                if c == 7:
                    mark_xnt_tr[t] = mk
                dv = add("vector", (lambda e, pt=pt, c=c: e.tensor_copy(
                    bb(XNR_O + s3 * 2048 + c * 256 + (t % 2) * 128, 128), pt[:, 0:128])),
                    waits=[("te", mk)], inc="dve")
                pool_tr.release(pt, "dve", dv)
                if c == 7 and t % 2 == 1:
                    mark_xn_evac[g] = plan.val("dve")
            push(u_tr)

    def push_p1_conv_pe(g):
        def u_conv():
            sA, sL, sR = g % 3, (g - 1) % 3, (g + 1) % 3
            waits = [("dve", mark_xn_evac[min(g + 1, 15)])]
            if g >= 1:
                waits.append(("te", mark_pe_mm.get(g - 1)))
                waits.append(("act", mark_cast.get(g - 1)))
            for c in range(8):
                add("vector", (lambda e, c=c: e.tensor_scalar(
                    fb(XS0_O + c * 256, 256), bb(XNR_O + sA * 2048 + c * 256, 256),
                    fb(TAP_O + c * 3 + 1, 1), None, OP.mult)),
                    waits=waits if c == 0 else [])
            for c in range(8):
                add("vector", (lambda e, c=c: e.scalar_tensor_tensor(
                    fb(XS0_O + c * 256 + 1, 255), bb(XNR_O + sA * 2048 + c * 256, 255),
                    fb(TAP_O + c * 3 + 0, 1), fb(XS0_O + c * 256 + 1, 255), OP.mult, OP.add)))
            if g > 0:
                for c in range(8):
                    add("vector", (lambda e, c=c: e.scalar_tensor_tensor(
                        fb(XS0_O + c * 256, 1), bb(XNR_O + sL * 2048 + c * 256 + 255, 1),
                        fb(TAP_O + c * 3 + 0, 1), fb(XS0_O + c * 256, 1), OP.mult, OP.add)))
            for c in range(8):
                add("vector", (lambda e, c=c: e.scalar_tensor_tensor(
                    fb(XS0_O + c * 256, 255), bb(XNR_O + sA * 2048 + c * 256 + 1, 255),
                    fb(TAP_O + c * 3 + 2, 1), fb(XS0_O + c * 256, 255), OP.mult, OP.add)))
            mconv = None
            if g < 15:
                for c in range(8):
                    mconv = add("vector", (lambda e, c=c: e.scalar_tensor_tensor(
                        fb(XS0_O + c * 256 + 255, 1), bb(XNR_O + sR * 2048 + c * 256, 1),
                        fb(TAP_O + c * 3 + 2, 1), fb(XS0_O + c * 256 + 255, 1),
                        OP.mult, OP.add)), inc="dve")
            else:
                mconv = add("vector", lambda e: e.engine_nop(), inc="dve")
            mark_cast[g] = add("scalar", (lambda e: e.activation(
                bb(XSB_O, 2048), fb(XS0_O, 2048), AF.Identity, bias=0.0, scale=1.0)),
                waits=[("dve", mconv), ("te", mark_pe_mm.get(g - 1))], inc="act")
        push(u_conv)

        for o in range(8):
            def u_pe(o=o):
                slot4 = g % 4
                h_old = g - 4
                ps, w = pool_m.take()
                mk = None
                for ifc in range(8):
                    mk = add("tensor", (lambda e, ps=ps, ifc=ifc: e.matmul(
                        ps[:, 0:256], bb(PEO_O + ifc * 1024 + o * 128, 128),
                        bb(XSB_O + ifc * 256, 256), start=(ifc == 0), stop=(ifc == 7))),
                        waits=(w + [("act", mark_cast[g]), ("w", mark_w_pe)]) if ifc == 0 else [],
                        inc="te" if ifc == 7 else None)
                if o == 7:
                    mark_pe_mm[g] = mk
                ew = [("te", mk)]
                if h_old >= 0 and o == 0:
                    ew.append(("te", mark_qkv_mm.get(h_old)))
                dv = add("vector", (lambda e, ps=ps: e.tensor_add(
                    bb(XSR_O + slot4 * 2048 + o * 256, 256), ps[:, 0:256],
                    fb(XS0_O + o * 256, 256))), waits=ew, inc="dve")
                pool_m.release(ps, "dve", dv)
                if o == 7:
                    mark_xs[g] = plan.val("dve")
            push(u_pe)

    def push_p2_qkv(h):
        xb = XSR_O + (h % 4) * 2048
        ko, qo, vo = K_OS[h % 2], Q_OS[h % 2], V_OS[h % 2]
        for m in range(8):
            def u_k(m=m):
                ps, w = pool_m.take()
                mk = None
                for ifc in range(8):
                    mk = add("tensor", (lambda e, ps=ps, ifc=ifc, m=m: e.matmul(
                        ps[:, 0:256], bb(QKV_O + ifc * 2560 + 512 + m * 128, 128),
                        bb(xb + ifc * 256, 256), start=(ifc == 0), stop=(ifc == 7))),
                        waits=(w + [("dve", mark_xs[h]), ("w", mark_w_qkv)]) if ifc == 0 else [],
                        inc="te" if ifc == 7 else None)
                dv = add("vector", (lambda e, ps=ps, m=m: e.tensor_copy(
                    bb(ko + m * 256, 256), ps[:, 0:256])),
                    waits=[("te", mk), ("te", mark_attn_done.get(h - 2))], inc="dve")
                pool_m.release(ps, "dve", dv)
            push(u_k)
        for al in range(8):
            def u_q(al=al):
                ps, w = pool_m.take()
                mk = None
                for ifc in range(8):
                    mk = add("tensor", (lambda e, ps=ps, ifc=ifc: e.matmul(
                        ps[0:64, 0:256], bb(QKV_O + ifc * 2560 + al * 64, 64),
                        bb(xb + ifc * 256, 256), start=(ifc == 0), stop=(ifc == 7))),
                        waits=w if ifc == 0 else [], inc="te" if ifc == 7 else None)
                dv = add("vector", (lambda e, ps=ps: e.tensor_copy(
                    bb(qo + al * 256, 256, 0, 64), ps[0:64, 0:256])),
                    waits=[("te", mk), ("te", mark_attn_done.get(h - 2))], inc="dve")
                pool_m.release(ps, "dve", dv)
                if al == 7:
                    mark_qe = plan.val("dve")
                    mark_qrep[h] = add("gpsimd", (lambda e: e.dma_start(
                        bb(qo, 2048, 64, 128), bb(qo, 2048, 0, 64))),
                        waits=[("dve", mark_qe), ("te", mark_attn_done.get(h - 2))],
                        inc="w", k=16)
            push(u_q)
        for j in range(2):
            for vc in range(2):
                def u_v(j=j, vc=vc):
                    ps, w = pool_m.take()
                    mk = None
                    for ifc in range(8):
                        ww = []
                        if ifc == 0:
                            ww = w + ([("g", mark_g_ones)] if (h <= 1) else [])
                        mk = add("tensor", (lambda e, ps=ps, ifc=ifc: e.matmul(
                            ps[:, 0:512], bb(xb + ifc * 256 + j * 128, 128),
                            bb(QKV_O + ifc * 2560 + 1536 + vc * 512, 512),
                            start=(ifc == 0), stop=(ifc == 7))),
                            waits=ww, inc="te" if ifc == 7 else None)
                    mark_qkv_mm[h] = mk
                    dv = add("vector", (lambda e, ps=ps: e.tensor_copy(
                        bass.AP(BB, vo + j * 1040 + vc * 520, [[NB, 128], [65, 8], [1, 64]]),
                        ps[:, 0:512])),
                        waits=[("te", mk), ("te", mark_attn_done.get(h - 2))], inc="dve")
                    pool_m.release(ps, "dve", dv)
                    if j == 1 and vc == 1:
                        mark_kqv[h] = plan.val("dve")
                push(u_v)

    mark_o_free = [None, None]
    mark_st_free = [None]

    st_exp_mark = [None, None]
    av_hist = {}
    exp_hist = {}
    gpair = [0]
    mark_o_tr = [None]

    def p2_attn(h):
        ko, qo, vo = K_OS[h % 2], Q_OS[h % 2], V_OS[h % 2]
        for sqc in range(4):
            ob = 0
            ps_o = ps_o0
            base = gpair[0]
            for pp in range(16):
                k = base + pp
                pst = ps_st0 if k % 2 == 0 else ps_st1
                stw = [("act", st_exp_mark[k % 2])] if st_exp_mark[k % 2] else []
                if pp == 0:
                    stw.append(("dve", mark_kqv[h]))
                    stw.append(("w", mark_qrep[h]))
                m2 = pp // 2
                jj = pp % 2
                add("tensor", (lambda e, m2=m2, jj=jj, sqc=sqc, ko=ko, qo=qo,
                     pst=pst: e.matmul(
                    pst[:, 0:512], bb(ko + m2 * 256 + jj * 128, 128, 0, 64),
                    bb(qo + sqc * 512, 512, 0, 64), start=True, stop=True,
                    tile_position=(0, 0))),
                    waits=stw)
                tep = add("tensor", (lambda e, m2=m2, jj=jj, sqc=sqc, ko=ko, qo=qo,
                          pst=pst: e.matmul(
                    pst[:, 512:1024], bb(ko + m2 * 256 + jj * 128, 128, 64, 128),
                    bb(qo + sqc * 512, 512, 64, 128), start=True, stop=True,
                    tile_position=(64, 0))), inc="te")
                ew = [("te", tep)]
                if (k - 4) in av_hist:
                    ew.append(("te", av_hist[k - 4]))
                exp_hist[k] = add("scalar", (lambda e, k=k, pst=pst: e.activation(
                    bb(PT_O + (k % 4) * 1024, 1024), pst[:, 0:1024], AF.Exp,
                    bias=0.0, scale=0.125)), waits=ew, inc="act")
                st_exp_mark[k % 2] = exp_hist[k]
                mark_st_free[0] = exp_hist[k]
                inject(1)
                if pp >= 2:
                    q = pp - 2
                    kq = base + q
                    qm, qj = q // 2, q % 2
                    aw = [("act", exp_hist[kq])]
                    if q == 0 and mark_o_free[ob]:
                        aw.append(("dve", mark_o_free[ob]))
                    add("tensor", (lambda e, qm=qm, qj=qj, q=q, kq=kq, vo=vo,
                         ps_o=ps_o: e.matmul(
                        ps_o[0:65, 0:512], bb(vo + qj * 1040 + (2 * qm) * 65, 65),
                        bb(PT_O + (kq % 4) * 1024, 512), start=(q == 0), stop=False)),
                        waits=aw)
                    av_hist[kq] = add("tensor", (lambda e, qm=qm, qj=qj, kq=kq,
                                      vo=vo, ps_o=ps_o: e.matmul(
                        ps_o[0:65, 0:512], bb(vo + qj * 1040 + (2 * qm + 1) * 65, 65),
                        bb(PT_O + (kq % 4) * 1024 + 512, 512), start=False, stop=False)),
                        inc="te")
                inject(1)
            for q in (14, 15):
                kq = base + q
                qm, qj = q // 2, q % 2
                add("tensor", (lambda e, qm=qm, qj=qj, kq=kq, vo=vo, ps_o=ps_o: e.matmul(
                    ps_o[0:65, 0:512], bb(vo + qj * 1040 + (2 * qm) * 65, 65),
                    bb(PT_O + (kq % 4) * 1024, 512), start=False, stop=False)),
                    waits=[("act", exp_hist[kq])])
                av_hist[kq] = add("tensor", (lambda e, qm=qm, qj=qj, q=q, kq=kq,
                                  vo=vo, ps_o=ps_o: e.matmul(
                    ps_o[0:65, 0:512], bb(vo + qj * 1040 + (2 * qm + 1) * 65, 65),
                    bb(PT_O + (kq % 4) * 1024 + 512, 512), start=False,
                    stop=(q == 15))), inc="te")
            gpair[0] = base + 16
            drain_tag("ofin")
            dvo = add("vector", (lambda e, ps_o=ps_o: e.tensor_copy(
                osb32(0, 512, 0, 65), ps_o[0:65, 0:512])),
                waits=[("te", av_hist[base + 15]), ("te", mark_o_tr[0])], inc="dve")
            mark_o_free[ob] = dvo
            for q128 in range(4):
                def u_ofin(q128=q128, sqc=sqc, h=h, dvo=dvo):
                    pt, w = pool_m.take()
                    ti = sqc * 4 + q128
                    mk = add("tensor", (lambda e, pt=pt, q128=q128: e.transpose(
                        pt[:, 0:65], osb32(q128 * 128, 128, 0, 65),
                        fb(EYE32_O, 65, 0, 65))),
                        waits=w + [("dve", dvo), ("act", mark_eye32)], inc="te")
                    mark_o_tr[0] = mk
                    if q128 == 3:
                        mark_attn_done[h] = mk
                    add("vector", (lambda e, pt=pt, q128=q128: e.reciprocal(
                        fb(RL_O + q128 % 2, 1), pt[:, 64:65])), waits=[("te", mk)])
                    add("vector", lambda e: e.drain())
                    dv = add("vector", (lambda e, pt=pt, ti=ti, h=h, q128=q128: e.tensor_scalar(
                        bb(ATW_O + ti * 1024 + h * 64, 64), pt[:, 0:64],
                        fb(RL_O + q128 % 2, 1), None, OP.mult)), inc="dve")
                    pool_m.release(pt, "dve", dv)
                push(u_ofin, tag="ofin")

    def push_router(ti):
        def u_ld():
            slot = ti % 3
            waits = [("act", mark_act_apply[31])]
            if ti >= 3:
                waits = [("act", mark_router_apply[ti - 3])]
            mark_x[("r", ti)] = add("sync", (lambda e: e.dma_start(
                fb(XT_O + slot * 1024, 1024), d_xr[ti * 128:(ti + 1) * 128, :])),
                waits=waits, inc="x", k=16)
        push(u_ld)

        def u_ln():
            slot = ti % 3
            b = ti % 2
            add("vector", (lambda e: e.bn_stats(
                fb(LNS_O + b * 16, 6), fb(XT_O + slot * 1024, 512))),
                waits=[("x", mark_x[("r", ti)]),
                       ("act", mark_router_apply.get(ti - 2) if ti >= 2
                        else mark_act_apply.get(31))])
            add("vector", (lambda e: e.bn_stats(
                fb(LNS_O + b * 16 + 6, 6), fb(XT_O + slot * 1024 + 512, 512))))
            add("vector", lambda e: e.drain())
            mdve = add("vector", (lambda e: e.bn_aggr(
                fb(LNS_O + 32 + b * 2, 2), fb(LNS_O + b * 16, 12))), inc="dve")
            add("scalar", (lambda e: e.activation(
                fb(LNS_O + 36 + b * 3, 1), fb(LNS_O + 32 + b * 2 + 1, 1), AF.Ln,
                bias=fb(EPSC_O, 1), scale=1.0)), waits=[("dve", mdve)])
            add("scalar", lambda e: e.drain())
            mact = add("scalar", (lambda e: e.activation(
                fb(LNS_O + 37 + b * 3, 1), fb(LNS_O + 36 + b * 3, 1),
                AF.Exp, bias=0.0, scale=-0.5)), inc="act")
            mdve = add("vector", (lambda e: e.scalar_tensor_tensor(
                fb(LNS_O + 38 + b * 3, 1), fb(LNS_O + 32 + b * 2, 1), -1.0,
                fb(LNS_O + 37 + b * 3, 1), OP.mult, OP.mult)),
                waits=[("act", mact)], inc="dve")
            mark_router_apply[ti] = add("scalar", (lambda e: e.activation(
                fb(XNF32_O, 1024), fb(XT_O + slot * 1024, 1024), AF.Identity,
                bias=fb(LNS_O + 38 + b * 3, 1), scale=fb(LNS_O + 37 + b * 3, 1))),
                waits=[("dve", mdve), ("dve", mark_xnfm.get(ti - 1))], inc="act")
            add("vector", (lambda e: e.tensor_copy(bb(XHI_O, 1024), fb(XNF32_O, 1024))),
                waits=[("act", mark_router_apply[ti]), ("te", mark_attn_done.get(15))])
            add("vector", lambda e: e.drain())
            add("vector", (lambda e: e.tensor_tensor(
                bb(XLOT_O, 1024), fb(XNF32_O, 1024), bb(XHI_O, 1024), OP.subtract)),
                inc="dve")
        push(u_ln)

        for cc in range(8):
            def u_tr(c=cc):
                mhl = plan.val("dve")
                pt, w = pool_tr.take()
                mk = add("tensor", (lambda e, pt=pt, c=c: e.transpose(
                    pt[:, 0:128], bb(XHI_O + c * 128, 128), bb(EYE_O, 128))),
                    waits=w + [("dve", mhl), ("te", mark_qkv_mm[15])], inc="te")
                dv = add("vector", (lambda e, pt=pt, c=c: e.tensor_copy(
                    bb(XNFM_O + c * 2048 + ti * 128, 128), pt[:, 0:128])),
                    waits=[("te", mk)], inc="dve")
                pool_tr.release(pt, "dve", dv)
                pt, w = pool_tr.take()
                mk = add("tensor", (lambda e, pt=pt, c=c: e.transpose(
                    pt[:, 0:128], bb(XLOT_O + c * 128, 128), bb(EYE_O, 128))),
                    waits=w, inc="te")
                dv = add("vector", (lambda e, pt=pt, c=c: e.tensor_copy(
                    bb(XLO_O + c * 128, 128), pt[:, 0:128])), waits=[("te", mk)], inc="dve")
                pool_tr.release(pt, "dve", dv)
                if c == 7:
                    mark_xnfm[ti] = dv
            push(u_tr)

        def u_mm():
            mtrans = mark_xnfm[ti]
            ps, w = pool_m.take()
            first = True
            for (lo_lhs, rhs_off) in ((0, WRH_O), (0, WRL_O), (1, WRH_O)):
                for ifc in range(8):
                    lhs_off = (XLO_O + ifc * 128) if lo_lhs else (XNFM_O + ifc * 2048 + ti * 128)
                    add("tensor", (lambda e, ps=ps, lo=lhs_off, ro=rhs_off, ifc=ifc,
                                   fr=first: e.matmul(
                        ps[:, 0:8], bb(lo, 128), bb(ro + ifc * 8, 8),
                        start=fr, stop=False)),
                        waits=(w + [("dve", mtrans)]) if first else [])
                    first = False
            mk = add("tensor", (lambda e, ps=ps: e.matmul(
                ps[:, 0:8], bb(ONE_O, 128, 0, 1), bb(BRB_O, 8, 0, 1),
                start=False, stop=True)), inc="te")
            dv = add("vector", (lambda e, ps=ps, ti=ti: e.tensor_copy(
                fb(LGT_O + ti * 8, 8), ps[:, 0:8])), waits=[("te", mk)], inc="dve")
            pool_m.release(ps, "dve", dv)
            mark_logits[ti] = dv
        push(u_mm)

    for it in range(18):
        if it < 16:
            push_p1_tile(it, 2 * it)
            push_p1_tile(it, 2 * it + 1)
        if 1 <= it <= 16:
            push_p1_conv_pe(it - 1)
        if 0 <= it - 1 < 16:
            push_p2_qkv(it - 1)
        if it == 17:
            for ti in range(16):
                push_router(ti)
        hh = it - 2
        if 0 <= hh < 16:
            p2_attn(hh)
        flush()

    mark_attn_all = plan.val("dve")

    def seg(off, n=16, k=8, bcast=False):
        if bcast:
            return bass.AP(FB, off, [[NF, 128], [1, n], [0, k]])
        return bass.AP(FB, off, [[NF, 128], [8, n], [1, k]])

    def seg1(off, n=16):
        return bass.AP(FB, off, [[NF, 128], [1, n]])

    mlg = mark_logits[15]
    add("vector", (lambda e: e.tensor_reduce(
        seg1(RSC_O), seg(LGT_O), AX, OP.max, negate=True)),
        waits=[("dve", mlg)])
    add("vector", lambda e: e.drain())
    mdv = add("vector", (lambda e: e.tensor_tensor(
        fb(RSC_O + 64, 128), fb(LGT_O, 128), seg(RSC_O, bcast=True), OP.add)), inc="dve")
    mex = add("scalar", (lambda e: e.activation(
        fb(RSC_O + 64, 128), fb(RSC_O + 64, 128), AF.Exp, bias=0.0, scale=1.0)),
        waits=[("dve", mdv)], inc="act")
    add("vector", (lambda e: e.tensor_reduce(
        seg1(RSC_O + 16), seg(RSC_O + 64), AX, OP.add)), waits=[("act", mex)])
    add("vector", lambda e: e.drain())
    add("vector", (lambda e: e.reciprocal(fb(RSC_O + 32, 16), fb(RSC_O + 16, 16))))
    add("vector", lambda e: e.drain())
    add("vector", (lambda e: e.tensor_tensor(
        fb(RSC_O + 64, 128), fb(RSC_O + 64, 128), seg(RSC_O + 32, bcast=True), OP.mult)))
    add("vector", lambda e: e.drain())
    add("vector", (lambda e: e.tensor_reduce(
        seg1(RSC_O), seg(RSC_O + 64), AX, OP.max)))
    add("vector", lambda e: e.drain())
    add("vector", (lambda e: e.tensor_tensor(
        fb(CWT_O, 128), fb(RSC_O + 64, 128), seg(RSC_O, bcast=True), OP.is_ge)))
    add("vector", lambda e: e.drain())
    add("vector", (lambda e: e.scalar_tensor_tensor(
        fb(CWT_O, 128), fb(CWT_O, 128), -1e30, fb(RSC_O + 64, 128), OP.mult, OP.add)))
    add("vector", lambda e: e.drain())
    add("vector", (lambda e: e.tensor_reduce(
        seg1(RSC_O), seg(CWT_O), AX, OP.max)))
    add("vector", lambda e: e.drain())
    add("vector", (lambda e: e.tensor_tensor(
        fb(CWT_O, 128), fb(RSC_O + 64, 128), seg(RSC_O, bcast=True), OP.is_ge)))
    add("vector", lambda e: e.drain())
    mcw = add("vector", (lambda e: e.tensor_tensor(
        fb(CWT_O, 128), fb(CWT_O, 128), fb(RSC_O + 64, 128), OP.mult)), inc="dve")
    add("vector", lambda e: e.drain())
    mcwb = add("vector", (lambda e: e.tensor_copy(bb(CWB_O, 128), fb(CWT_O, 128))),
               inc="dve")
    for ti in range(16):
        pt, w = pool_tr.take()
        mk = add("tensor", (lambda e, pt=pt, ti=ti: e.transpose(
            pt[0:8, 0:128], bb(CWB_O + ti * 8, 8), bb(EYE_O, 128))),
            waits=w + [("dve", mcwb)], inc="te")
        dv = add("vector", (lambda e, pt=pt, ti=ti: e.tensor_copy(
            bb(CWF_O + ti * 128, 128, 0, 8), pt[0:8, 0:128])),
            waits=[("te", mk)], inc="dve")
        pool_tr.release(pt, "dve", dv)
        mark_cw[ti] = dv

    add("gpsimd", lambda e: e.dma_start(
        bass.AP(BB, OWT_O, [[NB, 128], [1024, 8], [1, 1024]]),
        bass.AP(d_owT, 0, [[1024, 128], [1024 * 128, 8], [1, 1024]])),
        waits=[("te", mark_qkv_mm[15])], inc="w", k=16)
    mark_w_ow = plan.val("w")

    for it_ in pool_y.items:
        it_[1] = ("act", mark_st_free[0])

    mark_p3_te_last = None
    mark_acc_last = {}
    mark_p3_trans_last = None
    for ti in range(16):
        halves = []
        for half in range(2):
            key, w = pool_y.take()
            yt, yb = ybase(key)
            halves.append((half, key, yt, yb, w))
        dvs = []

        def emit_mm(cc, ti=ti):
            for half, key, yt, yb, wy in halves:
                waits = [("dve", dvs[cc])]
                if cc == 0:
                    waits += wy + [("w", mark_w_ow), ("dve", mark_cw[ti])]
                add("tensor", (lambda e, yt=yt, yb=yb, cc=cc, half=half: e.matmul(
                    yt[:, yb:yb + 512], bb(AFM_O + cc * 128, 128),
                    bb(OWT_O + cc * 1024 + half * 512, 512),
                    start=(cc == 0), stop=False)), waits=waits)

        for c in range(8):
            pt, w = pool_tr.take()
            mk = add("tensor", (lambda e, pt=pt, ti=ti, c=c: e.transpose(
                pt[:, 0:128], bb(ATW_O + ti * 1024 + c * 128, 128), bb(EYE_O, 128))),
                waits=w + ([("dve", mark_attn_all)] if (ti == 0 and c == 0) else []), inc="te")
            mark_p3_trans_last = mk
            dv = add("vector", (lambda e, pt=pt, c=c: e.tensor_copy(
                bb(AFM_O + c * 128, 128), pt[:, 0:128])), waits=[("te", mk)], inc="dve")
            pool_tr.release(pt, "dve", dv)
            dvs.append(dv)
            if c >= 1:
                emit_mm(c - 1)
        emit_mm(7)
        for half, key, yt, yb, wy in halves:
            mk = add("tensor", (lambda e, yt=yt, yb=yb, ti=ti, half=half: e.matmul(
                yt[:, yb:yb + 512], bb(CWF_O + ti * 128, 128, 0, 8),
                bb(B2_O + half * 512, 512, 0, 8), start=False, stop=True)), inc="te")
            mark_p3_te_last = mk
            dv = add("vector", (lambda e, yt=yt, yb=yb, ti=ti, half=half: e.tensor_copy(
                bb(ACC_O + ti * 1024 + half * 512, 512), yt[:, yb:yb + 512])),
                waits=[("te", mk)], inc="dve")
            pool_y.release(key, "dve", dv)
            mark_acc_last[(ti, half)] = dv

    mark_w1_mm_done = None
    mark_w2_mm_done = mark_p3_te_last
    mark_w1_load = None
    mark_w2_load = None
    first_w2 = True
    for ee in range(8):
        for s in [0]:
            w1_waits = [("te", mark_w1_mm_done)] if mark_w1_mm_done is not None else \
                [("te", mark_attn_done[15])]
            mark_w1_load = add("gpsimd", (lambda e, ee=ee: e.dma_start(
                bass.AP(BB, W1_O, [[NB, 128], [2048, 8], [1, 2048]]),
                bass.AP(d_w1, ee * D * D2, [[2048, 128], [2048 * 128, 8], [1, 2048]]))),
                waits=w1_waits, inc="w", k=16)
            w2_waits = [("te", mark_p3_trans_last if first_w2 else mark_w2_mm_done)]
            first_w2 = False
            mark_w2_load = add("gpsimd", (lambda e, ee=ee: e.dma_start(
                bass.AP(BB, ATW_O, [[NB, 128], [1024, 16], [1, 1024]]),
                bass.AP(d_w2, ee * D2 * D, [[1024, 128], [1024 * 128, 16], [1, 1024]]))),
                waits=w2_waits, inc="w", k=16)
            for tc in range(4):
                hmarks = []
                for d2c in range(16):
                    ps, w = pool_m.take()
                    mk = None
                    for ifc in range(8):
                        waits = []
                        if ifc == 0:
                            waits = w + [("w", mark_w1_load), ("dve", mark_xnfm.get(15))]
                        mk = add("tensor", (lambda e, ps=ps, d2c=d2c, ifc=ifc, s=s, tc=tc: e.matmul(
                            ps[:, 0:512], bb(W1_O + ifc * 2048 + d2c * 128, 128),
                            bb(XNFM_O + ifc * 2048 + tc * 512, 512),
                            start=(ifc == 0), stop=(ifc == 7))),
                            waits=waits, inc="te" if ifc == 7 else None)
                    mark_w1_mm_done = mk
                    hw = [("te", mk)]
                    if mark_w2_mm_done is not None:
                        hw.append(("te", mark_w2_mm_done))
                    hm = add("scalar", (lambda e, ps=ps, d2c=d2c, ee=ee: e.activation(
                        bb(PEO_O + d2c * 512, 512), ps[:, 0:512], AF.Relu,
                        bias=bb(B1C_O + d2c * 8 + ee, 1))), waits=hw, inc="act")
                    pool_m.release(ps, "act", hm)
                    hmarks.append(hm)
                for tj in range(4):
                    ti = tc * 4 + tj
                    for half in range(2):
                        key, w = pool_y.take()
                        yt, yb = ybase(key)
                        mk = None
                        for d2c in range(16):
                            waits = [("act", hmarks[d2c])]
                            if d2c == 0:
                                waits += w + [("w", mark_w2_load)]
                            mk = add("tensor", (lambda e, yt=yt, yb=yb, d2c=d2c, tj=tj,
                                     half=half: e.matmul(
                                yt[:, yb:yb + 512],
                                bb(PEO_O + d2c * 512 + tj * 128, 128),
                                bb(ATW_O + d2c * 1024 + half * 512, 512),
                                start=(d2c == 0), stop=(d2c == 15))),
                                waits=waits, inc="te" if d2c == 15 else None)
                        mark_w2_mm_done = mk
                        dv = add("vector", (lambda e, yt=yt, yb=yb, ti=ti, half=half, ee=ee:
                                 e.scalar_tensor_tensor(
                            bb(ACC_O + ti * 1024 + half * 512, 512),
                            yt[:, yb:yb + 512],
                            fb(CWT_O + ti * 8 + ee, 1), bb(ACC_O + ti * 1024 + half * 512, 512),
                            OP.mult, OP.add)),
                            waits=[("te", mk), ("dve", mark_acc_last[(ti, half)])], inc="dve")
                        mark_acc_last[(ti, half)] = dv
                        pool_y.release(key, "dve", dv)

    mark_out_stage = {}
    for ti in range(16):
        slot = ti % 3
        waits = [("act", mark_router_apply[15])]
        if ti >= 3:
            waits = [("dve", mark_out_stage[ti - 3])]
        mx = add("sync", (lambda e, ti=ti, slot=slot: e.dma_start(
            fb(XT_O + slot * 1024, 1024), d_xr[ti * 128:(ti + 1) * 128, :])),
            waits=waits, inc="x", k=16)
        ob = ti % 2
        dv = add("vector", (lambda e, slot=slot, ob=ob, ti=ti: e.tensor_tensor(
            fb(XS0_O + ob * 1024, 1024), fb(XT_O + slot * 1024, 1024),
            bb(ACC_O + ti * 1024, 1024), OP.add)),
            waits=[("x", mx), ("dve", mark_acc_last[(ti, 0)]), ("dve", mark_acc_last[(ti, 1)]),
                   ("out", None if ti < 2 else (ti - 1) * 16)],
            inc="dve")
        mark_out_stage[ti] = dv
        add("sync", (lambda e, ti=ti, ob=ob: e.dma_start(
            d_out[ti * 128:(ti + 1) * 128, :], fb(XS0_O + ob * 1024, 1024))),
            waits=[("dve", dv)], inc="out", k=16)
    add("sync", lambda e: e.nop(), waits=[("out", plan.val("out"))])

    simulate(plan)

    with st, nc.Block() as block:
        def replay(name):
            def f(eng):
                hw = {}
                for waits, fn, inc, k in plan.q[name]:
                    for sname, v in waits:
                        if hw.get(sname, 0) < v:
                            eng.wait_ge(sems[sname], v)
                            hw[sname] = v
                    inst = fn(eng)
                    if inc is not None:
                        inst.then_inc(sems[inc], k)
            return f

        block.tensor(replay("tensor"))
        block.vector(replay("vector"))
        block.scalar(replay("scalar"))
        block.gpsimd(replay("gpsimd"))
        block.sync(replay("sync"))

    return nc


_CACHE = {}


def kernel(L0_attn_x, L0_attn_conv_w, L0_attn_pe_w, L0_attn_qkv_w, L0_attn_out_w,
           L0_ffn_Wr, L0_ffn_br, L0_ffn_W1, L0_ffn_b1, L0_ffn_W2, L0_ffn_b2):
    x = np.ascontiguousarray(np.asarray(L0_attn_x, dtype=np.float32))
    qkv_w = np.asarray(L0_attn_qkv_w, dtype=np.float32)
    pe_wT = np.ascontiguousarray(np.asarray(L0_attn_pe_w, dtype=np.float32).T)
    out_wT = np.ascontiguousarray(np.asarray(L0_attn_out_w, dtype=np.float32).T)
    taps = np.ascontiguousarray(np.asarray(L0_attn_conv_w, dtype=np.float32)[:, 0, :, 1])
    wrT = np.ascontiguousarray(np.asarray(L0_ffn_Wr, dtype=np.float32).T)
    br = np.ascontiguousarray(np.asarray(L0_ffn_br, dtype=np.float32).reshape(1, NE))
    b1 = np.ascontiguousarray(np.asarray(L0_ffn_b1, dtype=np.float32))
    b2 = np.ascontiguousarray(np.asarray(L0_ffn_b2, dtype=np.float32))
    w1 = np.ascontiguousarray(np.asarray(L0_ffn_W1, dtype=np.float32))
    w2 = np.ascontiguousarray(np.asarray(L0_ffn_W2, dtype=np.float32))
    eye = np.eye(128, dtype=ml_dtypes.bfloat16)

    if "nc" not in _CACHE:
        _CACHE["nc"] = build_nc()
    nc = _CACHE["nc"]

    o = np.arange(R)
    in_maps = []
    idxs = []
    for c in range(8):
        b, p = c // 2, c % 2
        idx = 16 * (o % 256) + 8 * p + o // 256
        idxs.append((b, idx))
        qkvT_local = np.ascontiguousarray(
            np.concatenate([qkv_w[512 * p:512 * (p + 1)], qkv_w[1024:3072]], 0).T)
        in_maps.append({
            "x_full": np.ascontiguousarray(x[b]),
            "x_res": np.ascontiguousarray(x[b][idx]),
            "qkvT": qkvT_local,
            "peT": pe_wT, "owT": out_wT, "taps": taps, "wrT": wrT, "br": br,
            "b1": b1, "b2": b2, "w1": w1, "w2": w2, "eye": eye,
        })

    res = run_bass_kernel_spmd(nc, in_maps, core_ids=list(range(8)))
    out = np.empty((4, S, D), dtype=np.float32)
    for c in range(8):
        b, idx = idxs[c]
        out[b][idx] = np.asarray(res.results[c]["out"], dtype=np.float32)
    return out
